# version 12
# baseline (speedup 1.0000x reference)
import contextlib
import numpy as np
import ml_dtypes
import concourse.bass as bass
import concourse.mybir as mybir
from concourse.bass_utils import run_bass_kernel_spmd

F32 = mybir.dt.float32
BF16 = mybir.dt.bfloat16
AF = mybir.ActivationFunctionType
OP = mybir.AluOpType
AXX = mybir.AxisListType.X

ENGINES = ("sync", "vector", "scalar", "tensor", "gpsimd")

N_CORES = 8
N = 2048
D = 2048
DI = 8192
QT = 512
NQ = N // QT
LN2 = float(np.log(2.0))

DEBUG_TAPS = []
STOP_AFTER = None


class Buf:
    __slots__ = ("ap", "writer", "readers", "name")

    def __init__(self, ap, name=""):
        self.ap = ap
        self.writer = None
        self.readers = []
        self.name = name


NSLOT = 8


class PB:
    def __init__(self):
        self.items = {e: [] for e in ENGINES}
        self.dmacnt = {e: 0 for e in ENGINES}
        self.out_toks = []
        self.fence_set = set()

    def emit(self, eng, fn, reads=(), writes=(), dma=False, is_out=False):
        deps = set()

        def need(tok):
            if tok[0] != eng or tok[2]:
                return True
            return eng != "tensor"

        for b in reads:
            if b.writer and need(b.writer):
                deps.add(b.writer[:2])
        for b in writes:
            if b.writer and need(b.writer):
                deps.add(b.writer[:2])
            for rt in b.readers:
                if need(rt):
                    deps.add(rt[:2])
        for (fe, fi) in self.fence_set:
            if fe != eng or self.items[fe][fi][2]:
                deps.add((fe, fi))
        idx = len(self.items[eng])
        dmaidx = -1
        if dma:
            dmaidx = self.dmacnt[eng]
            self.dmacnt[eng] += 1
        self.items[eng].append([fn, deps, dma, dmaidx])
        tok = (eng, idx, dma)
        for b in reads:
            b.readers.append(tok)
        for b in writes:
            b.writer = tok
            b.readers = []
        if is_out:
            self.out_toks.append(tok)
        return tok

    def fence(self):
        self.fence_set = {(e, len(self.items[e]) - 1) for e in ENGINES if self.items[e]}

    def build(self, block, sems, dsems):
        need = {e: set() for e in ENGINES}
        for e in ENGINES:
            for it in self.items[e]:
                for (pe, pi) in it[1]:
                    need[pe].add(pi)
        for (pe, pi, _d) in self.out_toks:
            need[pe].add(pi)
        val = {}
        inc = {}
        for e in ENGINES:
            c = 0
            for i, it in enumerate(self.items[e]):
                if it[2]:
                    s = it[3] % NSLOT
                    k = it[3] // NSLOT
                    val[(e, i)] = ("d", e, s, 16 * (k + 1))
                    inc[(e, i)] = ("d", e, s)
                else:
                    c += 1
                    inc[(e, i)] = ("c", e)
                    val[(e, i)] = ("c", e, None, c)
        final_out = {}
        for (pe, pi, _d) in self.out_toks:
            key = val[(pe, pi)]
            fk = (key[0], key[1], key[2])
            final_out[fk] = max(final_out.get(fk, 0), key[3])

        def sem_of(key):
            if key[0] == "d":
                return dsems[key[1]][key[2]]
            return sems[key[1]]

        def mk(e):
            def run(eng):
                wm = {}
                def wait_key(kind, en, slot, v):
                    sk = (kind, en, slot)
                    if v > wm.get(sk, 0):
                        eng.wait_ge(dsems[en][slot] if kind == "d" else sems[en], v)
                        wm[sk] = v
                for i, (fn, deps, dma, dmaidx) in enumerate(self.items[e]):
                    for (pe, pi) in sorted(deps):
                        kind, en, slot, v = val[(pe, pi)]
                        wait_key(kind, en, slot, v)
                    if dma:
                        s = dmaidx % NSLOT
                        k = dmaidx // NSLOT
                        if k > 0:
                            wait_key("d", e, s, 16 * k)
                    instr = fn(eng)
                    ik = inc[(e, i)]
                    if dma:
                        instr.then_inc(dsems[e][dmaidx % NSLOT], 16)
                    elif ik is not None:
                        instr.then_inc(sems[e], 1)
                for (kind, en, slot), v in final_out.items():
                    if en == e:
                        wait_key(kind, en, slot, v)
            return run

        block.sync(mk("sync"))
        block.vector(mk("vector"))
        block.scalar(mk("scalar"))
        block.tensor(mk("tensor"))
        block.gpsimd(mk("gpsimd"))


def _ap(x):
    return x if isinstance(x, tuple) else (x, x.ap)


def build_program():
    nc = bass.Bass("TRN2", target_bir_lowering=False, detect_race_conditions=False)
    pb = PB()
    taps = {}

    def hin(name, shape, dt=F32):
        return Buf(nc.dram_tensor(name, shape, dt, kind="ExternalInput").ap(), name)

    def hout(name, shape, dt=F32):
        return Buf(nc.dram_tensor(name, shape, dt, kind="ExternalOutput").ap(), name)

    def hscr(name, shape, dt=F32):
        return Buf(nc.dram_tensor(name, shape, dt, kind="Internal").ap(), name)

    x_h = hin("x", [N, D])
    wq_h = hin("wq", [D, D])
    wk_h = hin("wk", [D, D])
    wv_h = hin("wv", [D, D])
    wo_h = hin("wo", [D, D])
    wf_h = hin("wf", [128, 256])
    wmod_h = hin("wmod", [D, 12288])
    wg_h = hin("wg", [DI, D])
    wu_h = hin("wu", [DI, D])
    wd_h = hin("wd", [D, DI])
    nw_h = hin("nwcol", [128, 16])
    c_h = hin("ccol", [128, 16])
    eye_h = hin("eye", [128, 128])
    eyeb_h = hin("eyeb", [128, 128], BF16)
    mask_h = hin("maskinf", [128, 128])
    ones128_h = hin("ones128", [128, 128])
    eye16_h = hin("eye16", [16, 16])
    onescol_h = hin("onescol", [128, 1])

    y_h = hout("y", [N, D])

    mod_s = hscr("mods", [1, 12288])
    x1_s = hscr("x1s", [N, D])
    pat_s = hscr("pats", [N, D])
    pdn_s = hscr("pdns", [N, D])
    qint_s = hscr("qints", [DI, N], BF16)
    wdq_s = hscr("wdqs", [D, DI], BF16)

    es_g = contextlib.ExitStack()

    def mk_sb(stack):
        def sb(name, shape, dt=F32):
            t = stack.enter_context(nc.sbuf_tensor("sb_" + name, shape, dt))
            return Buf(t.ap(), name)

        def sbn(name, n, shape, dt=F32):
            return [sb(f"{name}{i}", shape, dt) for i in range(n)]
        return sb, sbn

    sbg, sbng = mk_sb(es_g)

    eye = sbg("eye", [128, 128])
    eyeb = sbg("eyeb", [128, 128], BF16)
    gacol = sbg("gacol", [128, 16])
    gbcol = sbg("gbcol", [128, 16])
    statc = sbng("statc", 6, [128, 1])
    tq = tqb = tqbs = stg = xst = tqnW = tqbnW = None

    sems = {e: es_g.enter_context(nc.semaphore(f"sem_{e}")) for e in ENGINES}
    dsems = {e: [es_g.enter_context(nc.semaphore(f"dsem_{e}_{s}")) for s in range(NSLOT)]
             for e in ("sync", "gpsimd", "scalar", "vector")}
    dsems["tensor"] = []
    blk = es_g.enter_context(nc.Block())

    def dma(eng, dst, src, is_out=False, slow=False):
        db, dap = _ap(dst)
        sbb, sap = _ap(src)
        if slow:
            return pb.emit(eng, lambda EE, dap=dap, sap=sap: EE.dma_start(
                out=dap, in_=sap, allow_slow_non_contiguous=True),
                reads=(sbb,), writes=(db,), dma=True, is_out=is_out)
        return pb.emit(eng, lambda EE, dap=dap, sap=sap: EE.dma_start(out=dap, in_=sap),
                       reads=(sbb,), writes=(db,), dma=True, is_out=is_out)

    def s_act(out, in_, func, bias=None, scale=None):
        ob, oap = _ap(out)
        ib, iap = _ap(in_)
        reads = (ib,)
        kw = {}
        if bias is not None:
            if isinstance(bias, tuple):
                kw["bias"] = bias[1]
                reads += (bias[0],)
            else:
                kw["bias"] = bias
        if scale is not None:
            if isinstance(scale, tuple):
                kw["scale"] = scale[1]
                reads += (scale[0],)
            else:
                kw["scale"] = scale
        return pb.emit("scalar",
                       lambda EE, oap=oap, iap=iap, kw=kw: EE.activation(out=oap, in_=iap, func=func, **kw),
                       reads=reads, writes=(ob,))

    def s_copy(out, in_):
        ob, oap = _ap(out)
        ib, iap = _ap(in_)
        return pb.emit("scalar", lambda EE, oap=oap, iap=iap: EE.copy(out=oap, in_=iap),
                       reads=(ib,), writes=(ob,))

    def t_mm(out, lhsT, rhs, start, stop):
        ob, oap = _ap(out)
        lb, lap = _ap(lhsT)
        rb, rap = _ap(rhs)
        return pb.emit("tensor",
                       lambda EE, oap=oap, lap=lap, rap=rap, st=start, sp=stop:
                       EE.matmul(out=oap, lhsT=lap, rhs=rap, start=st, stop=sp),
                       reads=(lb, rb), writes=(ob,))

    def t_tr(out, in_, bf=False, ident=None):
        ob, oap = _ap(out)
        ib, iap = _ap(in_)
        idb = ident if ident is not None else (eyeb if bf else eye)
        return pb.emit("tensor",
                       lambda EE, oap=oap, iap=iap, idap=idb.ap:
                       EE.transpose(out=oap, in_=iap, identity=idap),
                       reads=(ib, idb), writes=(ob,))

    def v_tt(out, a, b, op, eng="vector"):
        ob, oap = _ap(out)
        ab, aap = _ap(a)
        bb, bap = _ap(b)
        return pb.emit(eng,
                       lambda EE, oap=oap, aap=aap, bap=bap, op=op:
                       EE.tensor_tensor(out=oap, in0=aap, in1=bap, op=op),
                       reads=(ab, bb), writes=(ob,))

    def v_ts(out, a, s1, op0, s2=None, op1=None, eng="vector"):
        ob, oap = _ap(out)
        ab, aap = _ap(a)
        reads = (ab,)
        if isinstance(s1, tuple):
            reads += (s1[0],)
            s1 = s1[1]
        if isinstance(s2, tuple):
            reads += (s2[0],)
            s2 = s2[1]
        if op1 is None:
            return pb.emit(eng,
                           lambda EE, oap=oap, aap=aap, s1=s1, op0=op0: EE.tensor_scalar(
                               out=oap, in0=aap, scalar1=s1, scalar2=None, op0=op0),
                           reads=reads, writes=(ob,))
        return pb.emit(eng,
                       lambda EE, oap=oap, aap=aap, s1=s1, s2=s2, op0=op0, op1=op1:
                       EE.tensor_scalar(out=oap, in0=aap, scalar1=s1, scalar2=s2,
                                        op0=op0, op1=op1),
                       reads=reads, writes=(ob,))

    def v_stt(out, in0, scalar, in1, op0, op1):
        ob, oap = _ap(out)
        a0, a0ap = _ap(in0)
        a1, a1ap = _ap(in1)
        reads = (a0, a1)
        if isinstance(scalar, tuple):
            reads += (scalar[0],)
            scalar = scalar[1]
        return pb.emit("vector",
                       lambda EE, oap=oap, a0ap=a0ap, a1ap=a1ap, s=scalar, op0=op0, op1=op1:
                       EE.scalar_tensor_tensor(out=oap, in0=a0ap, scalar=s, in1=a1ap,
                                               op0=op0, op1=op1),
                       reads=reads, writes=(ob,))

    def v_copy(out, in_, eng="vector"):
        ob, oap = _ap(out)
        ib, iap = _ap(in_)
        return pb.emit(eng, lambda EE, oap=oap, iap=iap: EE.tensor_copy(out=oap, in_=iap),
                       reads=(ib,), writes=(ob,))

    fired = set()

    def tap2(name, src):
        if name in DEBUG_TAPS and name not in fired:
            fired.add(name)
            tap(name, src)

    def dma_tr(dst, src, eng="sync"):
        db, dap = _ap(dst)
        sbb, sap = _ap(src)
        return pb.emit(eng, lambda EE, dap=dap, sap=sap: EE.dma_start(out=dap, in_=sap,
                                                                      transpose=True),
                       reads=(sbb,), writes=(db,), dma=True)

    def tap(name, src):
        if name in DEBUG_TAPS:
            sbb, sap = _ap(src)
            shape = list(sap.shape)
            t = hout("tap_" + name, shape, sap.dtype)
            taps[name] = t
            dma("gpsimd", t, (sbb, sap), is_out=True)

    from functools import partial
    qrr = [0]
    qpend = []

    def _emit_rr(chains):
        chains = [list(c) for c in chains if c]
        while chains:
            for c in list(chains):
                c.pop(0)()
                if not c:
                    chains.remove(c)

    def quantize_flush():
        _emit_rr(qpend)
        qpend.clear()

    def quantize_multi(jobs, W, mfrac, mmax, posts=None):
        nblk = W // 32
        J = len(jobs)
        par = qrr[0] % 2
        qrr[0] += 1
        T = J * nblk
        tW = tqnW[par]
        tS = tqbnW[par]

        def wa(k):
            return (tW, tW.ap[:, k * 64:k * 64 + T])
        m, e_, u0, tk, fl, rb = (wa(0), wa(1), wa(2), wa(3), wa(4), wa(5))
        sbw = (tS, tS.ap[:, 0:T])
        blk = []
        bs = blk.append
        for j, (vt, _o) in enumerate(jobs):
            vb, vap = _ap(vt)
            bs(partial(pb.emit, "vector",
                       lambda EE, oap2=tW.ap[:, j * nblk:j * nblk + nblk].unsqueeze(2),
                       iap=vap.rearrange("p (b c) -> p b c", c=32):
                       EE.tensor_reduce(out=oap2, in_=iap, axis=AXX, op=OP.max,
                                        apply_absolute_value=True),
                       reads=(vb,), writes=(tW,)))
        bs(partial(v_ts, m, m, 1e-30, OP.max))
        bs(partial(s_act, e_, m, AF.Ln))
        bs(partial(v_ts, e_, e_, 1.0 / LN2, OP.mult, 32.0, OP.add))
        bs(partial(v_copy, u0, e_))
        for k in (32.0, 16.0, 8.0, 4.0, 2.0, 1.0):
            bs(partial(v_ts, tk, u0, k, OP.is_ge, k, OP.mult))
            bs(partial(v_tt, u0, u0, tk, OP.subtract))
        bs(partial(v_tt, fl, e_, u0, OP.subtract))
        bs(partial(v_ts, tk, fl, -LN2, OP.mult, 34.0 * LN2, OP.add))
        bs(partial(s_act, rb, tk, AF.Exp))
        bs(partial(v_ts, tk, fl, LN2, OP.mult, -34.0 * LN2, OP.add))
        bs(partial(s_act, fl, tk, AF.Exp))
        bs(partial(v_copy, sbw, fl))

        delta = 2.0 ** -(mfrac + 1)
        lv = [4.0, 2.0, 1.0] + [2.0 ** -(jj + 1) for jj in range(mfrac)]
        chains = []
        for j, (vt, out_bf) in enumerate(jobs):
            vb, vap = _ap(vt)
            ob, oap = _ap(out_bf)
            rbj = (tW, tW.ap[:, 5 * 64 + j * nblk:5 * 64 + (j + 1) * nblk])
            sbj = (tS, tS.ap[:, j * nblk:(j + 1) * nblk])
            tb = par * 8 + j * 4
            A = (qtq[tb + 0], qtq[tb + 0].ap[:, 0:W])
            B = (qtq[tb + 1], qtq[tb + 1].ap[:, 0:W])
            C = (qtq[tb + 2], qtq[tb + 2].ap[:, 0:W])
            Dd = (qtq[tb + 3], qtq[tb + 3].ap[:, 0:W])
            st = []
            a = st.append
            a(partial(pb.emit, "vector",
                      lambda EE, oap2=A[1].rearrange("p (b c) -> p b c", c=32),
                      iap=vap.rearrange("p (b c) -> p b c", c=32),
                      rap=rbj[1].unsqueeze(2).broadcast_to([128, nblk, 32]):
                      EE.tensor_tensor(out=oap2, in0=iap, in1=rap, op=OP.mult),
                      reads=(vb, tW), writes=(A[0],)))
            a(partial(s_act, B, A, AF.Abs))
            a(partial(v_ts, C, B, 2.0, OP.is_ge))
            a(partial(v_ts, B, B, 4.0, OP.is_ge))
            a(partial(v_ts, C, C, -0.5, OP.mult, 1.0, OP.add))
            a(partial(v_ts, B, B, -0.5, OP.mult, 1.0, OP.add))
            a(partial(v_tt, B, C, B, OP.mult))
            a(partial(pb.emit, "vector", lambda EE, oap3=C[1], iap=B[1]:
                      EE.reciprocal(out=oap3, in_=iap),
                      reads=(B[0],), writes=(C[0],)))
            a(partial(v_tt, B, A, B, OP.mult))
            a(partial(v_ts, A, B, 2.0, OP.add))
            for k in lv:
                a(partial(v_ts, Dd, A, k - delta, OP.is_ge, k, OP.mult))
                a(partial(v_tt, A, A, Dd, OP.subtract))
            a(partial(v_tt, B, B, A, OP.subtract))
            a(partial(v_tt, B, B, C, OP.mult))
            a(partial(v_ts, B, B, float(mmax), OP.min, float(-mmax), OP.max))
            a(partial(pb.emit, "vector",
                      lambda EE, oap2=oap.rearrange("p (b c) -> p b c", c=32),
                      iap=B[1].rearrange("p (b c) -> p b c", c=32),
                      sap=sbj[1].unsqueeze(2).broadcast_to([128, nblk, 32]):
                      EE.tensor_tensor(out=oap2, in0=iap, in1=sap, op=OP.mult),
                      reads=(B[0], tS), writes=(ob,)))
            if posts:
                st.extend(posts[j])
            chains.append(st)
        prev = qpend[:]
        qpend.clear()
        _emit_rr(prev + [blk])
        qpend.extend(chains)

    es_q = contextlib.ExitStack()
    sbq, sbnq = mk_sb(es_q)
    gq = sbnq("gq", 8, [128, 1024])
    tqnW = [sbq("tqnWq0", [128, 384]), sbq("tqnWq1", [128, 384])]
    tqbnW = [sbq("tqbnq0", [128, 128], BF16), sbq("tqbnq1", [128, 128], BF16)]
    tqb = [sbq("tqbq0", [128, 1024], BF16), None,
           sbq("tqbq1", [128, 1024], BF16), None]
    tqbs = sbq("tqbsq", [128, 1024], BF16)
    wrow = sbnq("wrow", 2, [128, 2048])
    qtq = [gq[0], gq[1], gq[2], gq[3], None, None, None, None,
           gq[4], gq[5], gq[6], gq[7], None, None, None, None]

    qcall = [0]

    def _p6w_call(pc, jj):
        qb_t = tqb[(qrr[0] % 2) * 2]
        rt, qp = divmod(pc, 4)
        piece = qp * 2 + jj
        wdr = wrow[pc % 2]
        jobs = [((wdr, wdr.ap[:, jj * 1024:(jj + 1) * 1024]),
                 (qb_t, qb_t.ap[:, 0:1024]))]
        pl = []
        ci = qcall[0]
        qcall[0] += 1
        for t8 in range(8):
            Ic = piece * 8 + t8
            sl = ((t8 * 2 + ci % 2) % 8) * 128
            stg2 = (tqbs, tqbs.ap[:, sl:sl + 128])
            pl.append(partial(dma_tr, stg2,
                              (qb_t, qb_t.ap[:, t8 * 128:(t8 + 1) * 128]),
                              eng="scalar"))
            pl.append(partial(dma, "gpsimd",
                              (wdq_s, wdq_s.ap[rt * 128:(rt + 1) * 128,
                                               Ic * 128:(Ic + 1) * 128]),
                              stg2))
        quantize_multi(jobs, 1024, 1, 6.0, [pl])

    wq_thunks = []
    for pc in range(64):
        rt, qp = divmod(pc, 4)
        wq_thunks.append(partial(dma, "scalar", wrow[pc % 2],
                                 (wd_h, wd_h.ap[rt * 128:(rt + 1) * 128,
                                                qp * 2048:(qp + 1) * 2048])))
        wq_thunks.append(partial(_p6w_call, pc, 0))
        wq_thunks.append(partial(_p6w_call, pc, 1))

    def feed(n=1):
        for _ in range(n):
            if wq_thunks:
                wq_thunks.pop(0)()

    es_a = contextlib.ExitStack()
    sba, sbna = mk_sb(es_a)
    tq = sbna("tqA", 2, [128, 1024])
    tqc = sba("tqcA", [128, 16])
    stg = sbna("stgA", 1, [128, 512])
    xst = sbna("xstA", 1, [128, 2048])
    maskinf = sba("maskinf", [128, 128])
    ones128 = sba("ones128", [128, 128])
    eye16 = sba("eye16", [16, 16])
    onescol = sba("onescol", [128, 1])
    nwcol = sba("nwcol", [128, 16])
    ccol = sba("ccol", [128, 16])
    scT = sba("scT", [128, 16])
    diag16 = sba("diag16", [16, 16])
    sstate = sbna("sstate", 16, [128, 128])
    ldT = sba("ldT", [16, QT])
    cumT = sba("cumT", [16, QT])
    zeros16 = sba("zeros16", [16, 128])
    cumcol = sbna("cumcol", 4, [128, 16])
    ekcol = sbna("ekcol", 4, [128, 16])
    etot = sbna("etot", 4, [128, 16])
    ecol = sbna("ecol", 4, [128, 16])
    bc1 = sba("bc1", [128, 2048])
    bc2 = sba("bc2", [128, 2048])
    hT = sbna("hT", 16, [128, QT])
    oT = sbna("oT", 16, [128, QT])
    wstripe = sbna("wstripe", 2, [128, 2048])
    qkv = sbna("qkv", 3, [128, QT])
    att = sbna("att", 12, [128, 128])
    ktok = sbna("ktok", 2, [128, 128])
    vtok = sbna("vtok", 2, [128, 128])
    wmt = [Buf(gq[0].ap[:, 0:512], "wmt0"), Buf(gq[1].ap[:, 0:512], "wmt1")]
    wfb = sba("wfb", [128, 256])
    srow = sbna("srow", 2, [1, 512])

    ps = [Buf(es_a.enter_context(nc.psum_tensor(f"ps{i}", [128, 512], F32)).ap(), f"ps{i}")
          for i in range(8)]

    for dst, src in ((eye, eye_h), (eyeb, eyeb_h), (maskinf, mask_h), (ones128, ones128_h),
                     (eye16, eye16_h), (onescol, onescol_h), (nwcol, nw_h), (ccol, c_h),
                     (wfb, wf_h)):
        dma("sync", dst, src)
    pb.emit("gpsimd", lambda EE: EE.memset(zeros16.ap, 0.0), reads=(), writes=(zeros16,))
    for h in range(16):
        pb.emit("gpsimd", lambda EE, a=sstate[h].ap: EE.memset(a, 0.0),
                reads=(), writes=(sstate[h],))

    s_act(scT, ccol, AF.Silu)
    for jt in range(24):
        pbank = ps[jt % 2]
        for dc in range(16):
            wb = wmt[dc % 2]
            dma("sync", wb, (wmod_h, wmod_h.ap[dc * 128:(dc + 1) * 128, jt * 512:(jt + 1) * 512]))
            t_mm((pbank, pbank.ap[0:1, :]), (scT, scT.ap[:, dc:dc + 1]), wb,
                 start=(dc == 0), stop=(dc == 15))
        sr = (srow[jt % 2], srow[jt % 2].ap)
        s_copy(sr, (pbank, pbank.ap[0:1, :]))
        dma("gpsimd", (mod_s, mod_s.ap[:, jt * 512:(jt + 1) * 512]), sr)
    tap("modrow", (Buf(mod_s.ap, "m"), mod_s.ap))

    for col_out, idx in ((gacol, 2), (gbcol, 5)):
        raw = (tq[0], tq[0].ap[:, 0:16])
        dma("sync", raw, (mod_s, mod_s.ap[0:1, idx * 2048:(idx + 1) * 2048]
                          .rearrange("o (b c) -> (o c) b", c=128)), slow=True)
        t0 = (tq[1], tq[1].ap[:, 0:16])
        t1 = (tqc, tqc.ap[:, 0:16])
        v_ts(t0, raw, 0.0, OP.is_ge, 2.0, OP.mult)
        v_ts(t0, t0, -1.0, OP.add)
        v_ts(t1, raw, -1.0, OP.mult)
        v_tt(t1, raw, t1, OP.max)
        v_ts(t1, t1, 0.25, OP.max)
        v_tt(col_out, t0, t1, OP.mult)

    def bcast_mod(dst, idx, add1):
        src = mod_s.ap[0:1, idx * 2048:(idx + 1) * 2048].partition_broadcast(128).squeeze(1)
        dma("sync", dst, (mod_s, src))
        if add1:
            v_ts(dst, dst, 1.0, OP.add)

    bcast_mod(bc1, 1, True)
    bcast_mod(bc2, 0, False)
    tap("bc1", bc1)

    def ln1_q(qt):
        for icl in range(4):
            ic = qt * 4 + icl
            xt = xst[0]
            dma("sync", xt, (x_h, x_h.ap[ic * 128:(ic + 1) * 128, :]))
            s1 = (statc[0], statc[0].ap)
            s2 = (statc[1], statc[1].ap)
            for pp in range(2):
                xsl = (xt, xt.ap[:, pp * 1024:(pp + 1) * 1024])
                sq = (tq[0], tq[0].ap)
                s_act(sq, xsl, AF.Square)
                ps1 = (statc[2 + pp], statc[2 + pp].ap)
                pb.emit("vector", lambda EE, oap=ps1[1].unsqueeze(2), iap=xsl[1].unsqueeze(1):
                        EE.tensor_reduce(out=oap, in_=iap, axis=AXX, op=OP.add),
                        reads=(xt,), writes=(ps1[0],))
                ps2 = (statc[4 + pp], statc[4 + pp].ap)
                pb.emit("vector", lambda EE, oap=ps2[1].unsqueeze(2), iap=sq[1].unsqueeze(1):
                        EE.tensor_reduce(out=oap, in_=iap, axis=AXX, op=OP.add),
                        reads=(sq[0],), writes=(ps2[0],))
            v_tt(s1, (statc[2], statc[2].ap), (statc[3], statc[3].ap), OP.add)
            v_tt(s2, (statc[4], statc[4].ap), (statc[5], statc[5].ap), OP.add)
            mu = (statc[2], statc[2].ap)
            v_ts(mu, s1, 1.0 / D, OP.mult)
            mu2 = (statc[3], statc[3].ap)
            v_tt(mu2, mu, mu, OP.mult)
            var = (statc[4], statc[4].ap)
            v_ts(s2, s2, 1.0 / D, OP.mult)
            v_tt(var, s2, mu2, OP.subtract)
            v_ts(var, var, 1e-5, OP.add)
            rin = (statc[5], statc[5].ap)
            pb.emit("vector", lambda EE, oap=rin[1], iap=var[1]:
                    EE.reciprocal(out=oap, in_=iap), reads=(var[0],), writes=(rin[0],))
            tap2("dbg_mu2", statc[3])
            tap2("dbg_var", statc[4])
            tap2("dbg_rin", statc[5])
            s_act(rin, rin, AF.Sqrt)
            for pp in range(2):
                xsl = (xt, xt.ap[:, pp * 1024:(pp + 1) * 1024])
                xh = (tq[0], tq[0].ap)
                v_ts(xh, xsl, (statc[2], statc[2].ap), OP.subtract,
                     (statc[5], statc[5].ap), OP.mult)
                hh = (tq[1], tq[1].ap)
                v_tt(hh, xh, (bc1, bc1.ap[:, pp * 1024:(pp + 1) * 1024]), OP.mult)
                v_tt(hh, hh, (bc2, bc2.ap[:, pp * 1024:(pp + 1) * 1024]), OP.add)
                for f8 in range(8):
                    fc = pp * 8 + f8
                    pbank = ps[4 + (f8 % 2)]
                    pt = (pbank, pbank.ap[:, 0:128])
                    t_tr(pt, (hh[0], hh[1][:, f8 * 128:(f8 + 1) * 128]))
                    s_copy((hT[fc], hT[fc].ap[:, icl * 128:(icl + 1) * 128]), pt)

    def ld_q(qt):
        pbank = ps[6]
        for fc in range(16):
            t_mm((pbank, pbank.ap[0:16, :]),
                 (wfb, wfb.ap[:, fc * 16:(fc + 1) * 16]),
                 hT[fc], start=(fc == 0), stop=(fc == 15))
        s_act(ldT, (pbank, pbank.ap[0:16, :]), AF.Sigmoid)
        s_act(ldT, ldT, AF.Ln)
        for mc in range(4):
            pb.emit("vector",
                    lambda EE, oap=cumT.ap[0:16, mc * 128:(mc + 1) * 128],
                    iap=ldT.ap[0:16, mc * 128:(mc + 1) * 128], zap=zeros16.ap:
                    EE.tensor_tensor_scan(out=oap, data0=iap, data1=zap, initial=0.0,
                                          op0=OP.add, op1=OP.add),
                    reads=(ldT, zeros16), writes=(cumT,))
            pt = (ps[6], ps[6].ap[:, 0:16])
            t_tr(pt, (cumT, cumT.ap[0:16, mc * 128:(mc + 1) * 128]), ident=eye16)
            v_copy(cumcol[mc], pt)
            s_act(ecol[mc], cumcol[mc], AF.Exp)
            dg = (diag16, diag16.ap)
            pb.emit("vector",
                    lambda EE, oap=dg[1], iap=eye16.ap,
                    sap=cumT.ap[0:16, mc * 128 + 127:mc * 128 + 128]:
                    EE.tensor_scalar(out=oap, in0=iap, scalar1=sap, scalar2=None, op0=OP.mult),
                    reads=(eye16, cumT), writes=(diag16,))
            pt2 = (ps[6], ps[6].ap[:, 16:32])
            t_mm(pt2, (ones128, ones128.ap[0:16, 0:128]), dg, start=True, stop=True)
            s_act(etot[mc], pt2, AF.Exp)
            v_tt(ekcol[mc], pt2, cumcol[mc], OP.subtract)
            s_act(ekcol[mc], ekcol[mc], AF.Exp)

    def attn_q(qt):
        for h in range(16):
            for wi, (wh, dst, do_silu) in enumerate(((wq_h, qkv[0], True),
                                                     (wk_h, qkv[1], True),
                                                     (wv_h, qkv[2], False))):
                wsb = wstripe[wi % 2]
                dma("sync", wsb, (wh, wh.ap[h * 128:(h + 1) * 128, :]))
                pbank = ps[wi % 2]
                for fc in range(16):
                    t_mm(pbank, (wsb, wsb.ap[:, fc * 128:(fc + 1) * 128]),
                         hT[fc], start=(fc == 0), stop=(fc == 15))
                if do_silu:
                    s_act(dst, pbank, AF.Silu)
                else:
                    s_copy(dst, pbank)
            for mc in range(4):
                par = mc % 2
                qs_ = (qkv[0], qkv[0].ap[:, mc * 128:(mc + 1) * 128])
                ks_ = (qkv[1], qkv[1].ap[:, mc * 128:(mc + 1) * 128])
                vs_ = (qkv[2], qkv[2].ap[:, mc * 128:(mc + 1) * 128])
                pbank = ps[2]
                t_tr((pbank, pbank.ap[:, 0:128]), ks_)
                s_copy(ktok[par], (pbank, pbank.ap[:, 0:128]))
                t_tr((pbank, pbank.ap[:, 128:256]), vs_)
                s_copy(vtok[par], (pbank, pbank.ap[:, 128:256]))
                cc = att[par * 6 + 0]
                v_ts(cc, ones128, (cumcol[mc], cumcol[mc].ap[:, h:h + 1]), OP.mult)
                pA = (ps[3], ps[3].ap[:, 0:128])
                t_tr(pA, cc)
                ec = att[par * 6 + 1]
                v_ts(ec, ones128, (ecol[mc], ecol[mc].ap[:, h:h + 1]), OP.mult)
                pE = (ps[3], ps[3].ap[:, 128:256])
                t_tr(pE, ec)
                dd = att[par * 6 + 2]
                v_ts(dd, pA, (cumcol[mc], cumcol[mc].ap[:, h:h + 1]), OP.subtract)
                v_tt(dd, dd, maskinf, OP.add)
                s_act(dd, dd, AF.Exp)
                pS = (ps[4], ps[4].ap[:, 0:128])
                t_mm(pS, ks_, qs_, start=True, stop=True)
                st_ = att[par * 6 + 3]
                v_tt(st_, pS, dd, OP.mult)
                qd = att[par * 6 + 4]
                v_tt(qd, qs_, pE, OP.mult)
                pO = (ps[5], ps[5].ap[:, 0:128])
                t_mm(pO, vtok[par], st_, start=True, stop=False)
                t_mm(pO, sstate[h], qd, start=False, stop=True)
                s_copy((oT[h], oT[h].ap[:, mc * 128:(mc + 1) * 128]), pO)
                kd = att[par * 6 + 5]
                v_ts(kd, ktok[par], (ekcol[mc], ekcol[mc].ap[:, h:h + 1]), OP.mult)
                pST = (ps[7], ps[7].ap[:, 0:128])
                t_mm(pST, kd, vtok[par], start=True, stop=True)
                v_stt(sstate[h], sstate[h], (etot[mc], etot[mc].ap[:, h:h + 1]),
                      pST, OP.mult, OP.add)
                feed(1)

    def rms_q(qt):
        pR = (ps[6], ps[6].ap[0:1, :])
        for h in range(16):
            sq_ = (tq[0], tq[0].ap[:, 0:512])
            s_act(sq_, oT[h], AF.Square)
            t_mm(pR, onescol, sq_, start=(h == 0), stop=(h == 15))
        rrow = (srow[0], srow[0].ap)
        v_ts(rrow, pR, 1.0 / D, OP.mult, 1e-6, OP.add)
        pb.emit("vector", lambda EE, oap=rrow[1], iap=rrow[1]:
                EE.reciprocal(out=oap, in_=iap), reads=(srow[0],), writes=(srow[0],))
        s_act(rrow, rrow, AF.Sqrt)
        pRB = ps[3]
        t_mm(pRB, (ones128, ones128.ap[0:1, :]), rrow, start=True, stop=True)
        for h in range(16):
            v_stt(oT[h], oT[h], (nwcol, nwcol.ap[:, h:h + 1]), pRB, OP.mult, OP.mult)

    def wo_q(qt):
        for oc in range(16):
            wsb = wstripe[oc % 2]
            dma("sync", wsb, (wo_h, wo_h.ap[oc * 128:(oc + 1) * 128, :]))
            pbank = ps[oc % 2]
            for fc in range(16):
                t_mm(pbank, (wsb, wsb.ap[:, fc * 128:(fc + 1) * 128]),
                     oT[fc], start=(fc == 0), stop=(fc == 15))
            stg_ = stg[0]
            s_act(stg_, pbank, AF.Copy, scale=(gacol, gacol.ap[:, oc:oc + 1]))
            for t4 in range(4):
                pt = (ps[2], ps[2].ap[:, (t4 % 2) * 128:(t4 % 2) * 128 + 128])
                t_tr(pt, (stg_, stg_.ap[:, t4 * 128:(t4 + 1) * 128]))
                ic = qt * 4 + t4
                stg2 = (att[6 + t4 % 2], att[6 + t4 % 2].ap)
                v_copy(stg2, pt)
                dma("gpsimd", (pat_s, pat_s.ap[ic * 128:(ic + 1) * 128,
                                               oc * 128:(oc + 1) * 128]), stg2)
            feed(1)

    stage = [0]

    def go(f, *a):
        stage[0] += 1
        if STOP_AFTER is not None and stage[0] > STOP_AFTER:
            return
        f(*a)

    for qt in range(NQ):
        go(ln1_q, qt)
        if qt == 0:
            tap("bc1x", bc1)
            tap("hT0", hT[0])
            tap("hT15", hT[15])
            tap("dbg_s1", statc[0])
            tap("dbg_s2", statc[1])
            tap("dbg_mu", statc[2])
            tap("dbg_r", statc[5])
            tap("dbg_hh", tq[1])
        go(ld_q, qt)
        if qt == 0:
            tap("cumT0", cumT)
            tap("ekcol0", ekcol[3])
            tap("etot0", etot[3])
        go(attn_q, qt)
        if qt == 0:
            tap("qT0", qkv[0])
            tap("kT0", qkv[1])
            tap("vT0", qkv[2])
            tap("oTraw0", oT[0])
            tap("state0", sstate[0])
        go(rms_q, qt)
        if qt == 0:
            tap("oTn0", oT[0])
        go(wo_q, qt)
    while wq_thunks:
        feed(1)
    es_a.close()
    pb.fence()

    es_b = contextlib.ExitStack()
    sbb_, sbnb = mk_sb(es_b)
    tqx = sbnb("tqxB", 8, [128, 1024])
    qtq = [gq[0], gq[1], gq[2], gq[3], tqx[0], tqx[1], tqx[2], tqx[3],
           gq[4], gq[5], gq[6], gq[7], tqx[4], tqx[5], tqx[6], tqx[7]]
    tqb = [tqb[0], sbb_("tqbB0", [128, 1024], BF16),
           tqb[2], sbb_("tqbB1", [128, 1024], BF16)]

    psB = [Buf(es_b.enter_context(nc.psum_tensor(f"psB{i}", [128, 512], F32)).ap(), f"psB{i}")
           for i in range(6)]
    psBb = Buf(es_b.enter_context(nc.psum_tensor("psBb", [128, 512], BF16)).ap(), "psBb")
    qaT = sbnb("qaT", 16, [128, N], BF16)
    es_b1 = contextlib.ExitStack()
    sbb1, sbnb1 = mk_sb(es_b1)
    xst = sbnb1("xstB", 2, [128, 2048])
    bc1B = sbb1("bc1B", [128, 2048])
    bc2B = sbb1("bc2B", [128, 2048])
    x1t = sbb1("x1t", [128, 2048])
    pstg = sbb1("pstg", [128, 1024])

    def p5():
        bcast_mod(bc1B, 4, True)
        bcast_mod(bc2B, 3, False)
        for ic in range(16):
            xt = xst[ic % 2]
            dma("sync", xt, (x_h, x_h.ap[ic * 128:(ic + 1) * 128, :]))
            x1 = (x1t, x1t.ap)
            for pp in range(2):
                pcs = (pstg, pstg.ap[:, 0:1024])
                dma("sync", pcs, (pat_s, pat_s.ap[ic * 128:(ic + 1) * 128,
                                                  pp * 1024:(pp + 1) * 1024]))
                v_tt((x1t, x1t.ap[:, pp * 1024:(pp + 1) * 1024]),
                     (xt, xt.ap[:, pp * 1024:(pp + 1) * 1024]), pcs, OP.add)
            dma("gpsimd", (x1_s, x1_s.ap[ic * 128:(ic + 1) * 128, :]), x1)
            s1 = (statc[0], statc[0].ap)
            s2 = (statc[1], statc[1].ap)
            for pp in range(2):
                xsl = (x1t, x1t.ap[:, pp * 1024:(pp + 1) * 1024])
                sq = (pstg, pstg.ap)
                s_act(sq, xsl, AF.Square)
                ps1 = (statc[2 + pp], statc[2 + pp].ap)
                pb.emit("vector", lambda EE, oap=ps1[1].unsqueeze(2), iap=xsl[1].unsqueeze(1):
                        EE.tensor_reduce(out=oap, in_=iap, axis=AXX, op=OP.add),
                        reads=(x1t,), writes=(ps1[0],))
                ps2 = (statc[4 + pp], statc[4 + pp].ap)
                pb.emit("vector", lambda EE, oap=ps2[1].unsqueeze(2), iap=sq[1].unsqueeze(1):
                        EE.tensor_reduce(out=oap, in_=iap, axis=AXX, op=OP.add),
                        reads=(sq[0],), writes=(ps2[0],))
            v_tt(s1, (statc[2], statc[2].ap), (statc[3], statc[3].ap), OP.add)
            v_tt(s2, (statc[4], statc[4].ap), (statc[5], statc[5].ap), OP.add)
            v_ts((statc[2], statc[2].ap), s1, 1.0 / D, OP.mult)
            v_tt((statc[3], statc[3].ap), (statc[2], statc[2].ap),
                 (statc[2], statc[2].ap), OP.mult)
            v_ts(s2, s2, 1.0 / D, OP.mult)
            v_tt((statc[4], statc[4].ap), s2, (statc[3], statc[3].ap), OP.subtract)
            v_ts((statc[4], statc[4].ap), (statc[4], statc[4].ap), 1e-5, OP.add)
            pb.emit("vector", lambda EE, oap=statc[5].ap, iap=statc[4].ap:
                    EE.reciprocal(out=oap, in_=iap), reads=(statc[4],), writes=(statc[5],))
            s_act((statc[5], statc[5].ap), (statc[5], statc[5].ap), AF.Sqrt)
            jobs = []
            posts = []
            h2b = xst[ic % 2]
            par = qrr[0] % 2
            for pp in range(2):
                xsl = (x1t, x1t.ap[:, pp * 1024:(pp + 1) * 1024])
                h2 = (h2b, h2b.ap[:, pp * 1024:(pp + 1) * 1024])
                v_ts(h2, xsl, (statc[2], statc[2].ap), OP.subtract,
                     (statc[5], statc[5].ap), OP.mult)
                v_tt(h2, h2, (bc1B, bc1B.ap[:, pp * 1024:(pp + 1) * 1024]), OP.mult)
                v_tt(h2, h2, (bc2B, bc2B.ap[:, pp * 1024:(pp + 1) * 1024]), OP.add)
                qbj = tqb[par * 2 + pp]
                jobs.append((h2, (qbj, qbj.ap[:, 0:1024])))
                posts.append([partial(dma_tr,
                                      (qaT[pp * 8 + f8],
                                       qaT[pp * 8 + f8].ap[:, ic * 128:(ic + 1) * 128]),
                                      (qbj, qbj.ap[:, f8 * 128:(f8 + 1) * 128]))
                              for f8 in range(8)])
            quantize_multi(jobs, 1024, 3, 7.5, posts)
        quantize_flush()

    def p6a():
        for obg in range(16):
            for phase, (wsrc, is_gate) in enumerate(((wg_h, True), (wu_h, False))):
                qg = qgT[0]
                for r4 in range(4):
                    wt = wrow[r4 % 2]
                    wtv = (wt, wt.ap)
                    dma("sync", wtv, (wsrc, wsrc.ap[(obg * 4 + r4) * 128:
                                                    (obg * 4 + r4 + 1) * 128, :]))
                    par = qrr[0] % 2
                    jobs = []
                    posts = []
                    for pp in range(2):
                        qbj = tqb[par * 2 + pp]
                        jobs.append(((wt, wt.ap[:, pp * 1024:(pp + 1) * 1024]),
                                     (qbj, qbj.ap[:, 0:1024])))
                        posts.append([partial(dma_tr,
                                              (qg[pp * 8 + f8],
                                               qg[pp * 8 + f8].ap[:, r4 * 128:(r4 + 1) * 128]),
                                              (qbj, qbj.ap[:, f8 * 128:(f8 + 1) * 128]))
                                      for f8 in range(8)])
                    quantize_multi(jobs, 1024, 1, 6.0, posts)
                quantize_flush()
                for icq in range(4):
                    for i4 in range(4):
                        ic = icq * 4 + i4
                        for fc in range(16):
                            t_mm(psB[i4], (qaT[fc], qaT[fc].ap[:, ic * 128:(ic + 1) * 128]),
                                 qg[fc], start=(fc == 0), stop=(fc == 15))
                        if is_gate:
                            s_act(gstripe[ic], psB[i4], AF.Silu)
                        else:
                            v_tt(gstripe[ic], psB[i4], gstripe[ic], OP.mult)
            for icp in range(8):
                par = qrr[0] % 2
                jobs = []
                posts = []
                for jj in range(2):
                    ic = icp * 2 + jj
                    qbj = tqb[par * 2 + jj]
                    jobs.append(((gstripe[ic], gstripe[ic].ap),
                                 (qbj, qbj.ap[:, 0:512])))
                    pl = []
                    for t4 in range(4):
                        sl = (jj * 4 + t4) * 128
                        stg2 = (tqbs, tqbs.ap[:, sl:sl + 128])
                        pl.append(partial(dma_tr, stg2,
                                          (qbj, qbj.ap[:, t4 * 128:(t4 + 1) * 128])))
                        irow = (obg * 4 + t4) * 128
                        pl.append(partial(dma, "gpsimd",
                                          (qint_s, qint_s.ap[irow:irow + 128,
                                                             ic * 128:(ic + 1) * 128]),
                                          stg2))
                    posts.append(pl)
                quantize_multi(jobs, 512, 3, 7.5, posts)
        quantize_flush()

    go(p5)
    tap("x1t0", x1t)
    tap("qaT0", qaT[0])
    tap("qaT15", qaT[15])
    es_b1.close()
    pb.fence()
    es_b2 = contextlib.ExitStack()
    sbb2, sbnb2 = mk_sb(es_b2)
    gstripe = sbnb2("gstripe", 16, [128, 512])
    qgT = [sbnb2("qgT", 16, [128, 512], BF16)]
    go(p6a)
    es_b2.close()
    tap("gs0", gstripe[0])
    tap("qgT0", qgT[0])
    es_b.close()
    pb.fence()

    es_q.close()
    pb.fence()

    es_d = contextlib.ExitStack()
    sbd, sbnd = mk_sb(es_d)
    tq = sbnd("tqD", 2, [128, 1024])
    stg = sbnd("stgD", 2, [128, 512])
    xst = sbnd("xstD", 2, [128, 2048])
    psD = [Buf(es_d.enter_context(nc.psum_tensor(f"psD{i}", [128, 512], F32)).ap(), f"psD{i}")
           for i in range(6)]
    qiT = sbnd("qiT", 64, [128, 1024], BF16)
    wdq_sb = sbnd("wdq_sb", 2, [128, 8192], BF16)
    attD = sbnd("attD", 2, [128, 128])

    def p6b():
        for th in range(2):
            for Ic in range(64):
                dma("sync", qiT[Ic], (qint_s, qint_s.ap[Ic * 128:(Ic + 1) * 128,
                                                        th * 1024:(th + 1) * 1024]))
            for oc in range(16):
                wq_ = wdq_sb[oc % 2]
                dma("sync", wq_, (wdq_s, wdq_s.ap[oc * 128:(oc + 1) * 128, :]))
                pb0 = psD[(oc % 2) * 2]
                pb1 = psD[(oc % 2) * 2 + 1]
                for h5 in range(2):
                    bank = pb0 if h5 == 0 else pb1
                    for Ic in range(64):
                        t_mm(bank,
                             (wq_, wq_.ap[:, Ic * 128:(Ic + 1) * 128]),
                             (qiT[Ic], qiT[Ic].ap[:, h5 * 512:(h5 + 1) * 512]),
                             start=(Ic == 0), stop=(Ic == 63))
                stg_ = stg[oc % 2]
                for h5 in range(2):
                    s_act(stg_, pb0 if h5 == 0 else pb1, AF.Copy,
                          scale=(gbcol, gbcol.ap[:, oc:oc + 1]))
                    for t4 in range(4):
                        pt = (psD[4 + (t4 % 2)], psD[4 + (t4 % 2)].ap[:, 0:128])
                        t_tr(pt, (stg_, stg_.ap[:, t4 * 128:(t4 + 1) * 128]))
                        stg2 = attD[t4 % 2]
                        v_copy(stg2, pt)
                        ic = th * 8 + h5 * 4 + t4
                        dma("gpsimd", (pdn_s, pdn_s.ap[ic * 128:(ic + 1) * 128,
                                                       oc * 128:(oc + 1) * 128]), stg2)

    def p7():
        for ic in range(16):
            xt = xst[ic % 2]
            dma("sync", xt, (x1_s, x1_s.ap[ic * 128:(ic + 1) * 128, :]))
            for pp in range(2):
                pcs = (tq[0], tq[0].ap[:, 0:1024])
                dma("sync", pcs, (pdn_s, pdn_s.ap[ic * 128:(ic + 1) * 128,
                                                  pp * 1024:(pp + 1) * 1024]))
                yt = (tq[1], tq[1].ap[:, 0:1024])
                v_tt(yt, (xt, xt.ap[:, pp * 1024:(pp + 1) * 1024]), pcs, OP.add)
                dma("gpsimd", (y_h, y_h.ap[ic * 128:(ic + 1) * 128,
                                           pp * 1024:(pp + 1) * 1024]), yt, is_out=True)

    go(p6b)
    go(p7)
    es_d.close()

    pb.build(blk, sems, dsems)
    es_g.close()
    ni = sum(len(v) for v in pb.items.values())
    return nc, taps, ni


_CACHE = {}


def _host_inputs(L0_attn_x, L0_attn_Wq, L0_attn_Wk, L0_attn_Wv, L0_attn_Wf,
                 L0_attn_Wo, L0_attn_norm_w, L0_ffn_w_gate, L0_ffn_w_up,
                 L0_ffn_w_down, L0_c, L0_wmod):
    f32 = np.float32

    def stripes(W):
        return np.ascontiguousarray(
            np.asarray(W, f32).reshape(16, 128, 16, 128).transpose(2, 1, 0, 3)
            .reshape(2048, 2048))

    shared = {
        "wq": stripes(L0_attn_Wq),
        "wk": stripes(L0_attn_Wk),
        "wv": stripes(L0_attn_Wv),
        "wo": stripes(L0_attn_Wo),
        "wf": np.ascontiguousarray(
            np.asarray(L0_attn_Wf, f32).reshape(16, 128, 16).transpose(1, 0, 2)
            .reshape(128, 256)),
        "wmod": np.ascontiguousarray(np.asarray(L0_wmod, f32)),
        "wg": np.ascontiguousarray(np.asarray(L0_ffn_w_gate, f32)),
        "wu": np.ascontiguousarray(np.asarray(L0_ffn_w_up, f32)),
        "wd": np.ascontiguousarray(np.asarray(L0_ffn_w_down, f32)),
        "nwcol": np.ascontiguousarray(np.asarray(L0_attn_norm_w, f32).reshape(16, 128).T),
        "eye": np.eye(128, dtype=f32),
        "eyeb": np.eye(128, dtype=f32).astype(ml_dtypes.bfloat16),
        "maskinf": np.where(np.arange(128)[:, None] <= np.arange(128)[None, :],
                            0.0, -1e30).astype(f32),
        "ones128": np.ones((128, 128), f32),
        "eye16": np.eye(16, dtype=f32),
        "onescol": np.ones((128, 1), f32),
    }
    xs = np.asarray(L0_attn_x, f32)
    cs = np.asarray(L0_c, f32)
    in_maps = []
    for c in range(N_CORES):
        m = dict(shared)
        m["x"] = np.ascontiguousarray(xs[c])
        m["ccol"] = np.ascontiguousarray(cs[c].reshape(16, 128).T)
        in_maps.append(m)
    return in_maps


def kernel(**inputs):
    in_maps = _host_inputs(**inputs)
    key = ("prog", STOP_AFTER, tuple(DEBUG_TAPS))
    if key not in _CACHE:
        _CACHE[key] = build_program()
    nc, taps, _ni = _CACHE[key]
    res = run_bass_kernel_spmd(nc, in_maps, core_ids=list(range(N_CORES)))
    out = np.stack([np.asarray(r["y"]) for r in res.results], axis=0)
    if DEBUG_TAPS:
        kernel.last_taps = [{k: np.asarray(r["tap_" + k]) for k in DEBUG_TAPS}
                            for r in res.results]
    return out.astype(np.float32)


# version 14
# speedup vs baseline: 1.0515x; 1.0515x over previous
import contextlib
import numpy as np
import ml_dtypes
import concourse.bass as bass
import concourse.mybir as mybir
from concourse.bass_utils import run_bass_kernel_spmd

F32 = mybir.dt.float32
BF16 = mybir.dt.bfloat16
AF = mybir.ActivationFunctionType
OP = mybir.AluOpType
AXX = mybir.AxisListType.X

ENGINES = ("sync", "vector", "scalar", "tensor", "gpsimd")

N_CORES = 8
N = 2048
D = 2048
DI = 8192
QT = 512
NQ = N // QT
LN2 = float(np.log(2.0))

DEBUG_TAPS = []
STOP_AFTER = None


class Buf:
    __slots__ = ("ap", "writer", "readers", "name")

    def __init__(self, ap, name=""):
        self.ap = ap
        self.writer = None
        self.readers = []
        self.name = name


NSLOT = 8


class PB:
    def __init__(self):
        self.items = {e: [] for e in ENGINES}
        self.dmacnt = {e: 0 for e in ENGINES}
        self.out_toks = []
        self.fence_set = set()

    def emit(self, eng, fn, reads=(), writes=(), dma=False, is_out=False):
        deps = set()

        def need(tok):
            if tok[0] != eng or tok[2]:
                return True
            return eng != "tensor"

        for b in reads:
            if b.writer and need(b.writer):
                deps.add(b.writer[:2])
        for b in writes:
            if b.writer and need(b.writer):
                deps.add(b.writer[:2])
            for rt in b.readers:
                if need(rt):
                    deps.add(rt[:2])
        for (fe, fi) in self.fence_set:
            if fe != eng or self.items[fe][fi][2]:
                deps.add((fe, fi))
        idx = len(self.items[eng])
        dmaidx = -1
        if dma:
            dmaidx = self.dmacnt[eng]
            self.dmacnt[eng] += 1
        self.items[eng].append([fn, deps, dma, dmaidx])
        tok = (eng, idx, dma)
        for b in reads:
            b.readers.append(tok)
        for b in writes:
            b.writer = tok
            b.readers = []
        if is_out:
            self.out_toks.append(tok)
        return tok

    def fence(self):
        self.fence_set = {(e, len(self.items[e]) - 1) for e in ENGINES if self.items[e]}

    def build(self, block, sems, dsems):
        need = {e: set() for e in ENGINES}
        for e in ENGINES:
            for it in self.items[e]:
                for (pe, pi) in it[1]:
                    need[pe].add(pi)
        for (pe, pi, _d) in self.out_toks:
            need[pe].add(pi)
        val = {}
        inc = {}
        for e in ENGINES:
            c = 0
            for i, it in enumerate(self.items[e]):
                if it[2]:
                    s = it[3] % NSLOT
                    k = it[3] // NSLOT
                    val[(e, i)] = ("d", e, s, 16 * (k + 1))
                    inc[(e, i)] = ("d", e, s)
                else:
                    c += 1
                    inc[(e, i)] = ("c", e)
                    val[(e, i)] = ("c", e, None, c)
        final_out = {}
        for (pe, pi, _d) in self.out_toks:
            key = val[(pe, pi)]
            fk = (key[0], key[1], key[2])
            final_out[fk] = max(final_out.get(fk, 0), key[3])

        def sem_of(key):
            if key[0] == "d":
                return dsems[key[1]][key[2]]
            return sems[key[1]]

        def mk(e):
            def run(eng):
                wm = {}
                def wait_key(kind, en, slot, v):
                    sk = (kind, en, slot)
                    if v > wm.get(sk, 0):
                        eng.wait_ge(dsems[en][slot] if kind == "d" else sems[en], v)
                        wm[sk] = v
                for i, (fn, deps, dma, dmaidx) in enumerate(self.items[e]):
                    for (pe, pi) in sorted(deps):
                        kind, en, slot, v = val[(pe, pi)]
                        wait_key(kind, en, slot, v)
                    if dma:
                        s = dmaidx % NSLOT
                        k = dmaidx // NSLOT
                        if k > 0:
                            wait_key("d", e, s, 16 * k)
                    instr = fn(eng)
                    ik = inc[(e, i)]
                    if dma:
                        instr.then_inc(dsems[e][dmaidx % NSLOT], 16)
                    elif ik is not None:
                        instr.then_inc(sems[e], 1)
                for (kind, en, slot), v in final_out.items():
                    if en == e:
                        wait_key(kind, en, slot, v)
            return run

        block.sync(mk("sync"))
        block.vector(mk("vector"))
        block.scalar(mk("scalar"))
        block.tensor(mk("tensor"))
        block.gpsimd(mk("gpsimd"))


def _ap(x):
    return x if isinstance(x, tuple) else (x, x.ap)


def build_program():
    nc = bass.Bass("TRN2", target_bir_lowering=False, detect_race_conditions=False)
    pb = PB()
    taps = {}

    def hin(name, shape, dt=F32):
        return Buf(nc.dram_tensor(name, shape, dt, kind="ExternalInput").ap(), name)

    def hout(name, shape, dt=F32):
        return Buf(nc.dram_tensor(name, shape, dt, kind="ExternalOutput").ap(), name)

    def hscr(name, shape, dt=F32):
        return Buf(nc.dram_tensor(name, shape, dt, kind="Internal").ap(), name)

    x_h = hin("x", [N, D])
    wq_h = hin("wq", [D, D])
    wk_h = hin("wk", [D, D])
    wv_h = hin("wv", [D, D])
    wo_h = hin("wo", [D, D])
    wf_h = hin("wf", [128, 256])
    wmod_h = hin("wmod", [D, 12288])
    wg_h = hin("wg", [DI, D])
    wu_h = hin("wu", [DI, D])
    wd_h = hin("wd", [D, DI])
    nw_h = hin("nwcol", [128, 16])
    c_h = hin("ccol", [128, 16])
    eye_h = hin("eye", [128, 128])
    eyeb_h = hin("eyeb", [128, 128], BF16)
    mask_h = hin("maskinf", [128, 128])
    ones128_h = hin("ones128", [128, 128])
    eye16_h = hin("eye16", [16, 16])
    onescol_h = hin("onescol", [128, 1])

    y_h = hout("y", [N, D])

    mod_s = hscr("mods", [1, 12288])
    x1_s = hscr("x1s", [N, D])
    pat_s = hscr("pats", [N, D])
    pdn_s = hscr("pdns", [N, D])
    qint_s = hscr("qints", [DI, N], BF16)
    wdq_s = hscr("wdqs", [D, DI], BF16)

    es_g = contextlib.ExitStack()

    def mk_sb(stack):
        def sb(name, shape, dt=F32):
            t = stack.enter_context(nc.sbuf_tensor("sb_" + name, shape, dt))
            return Buf(t.ap(), name)

        def sbn(name, n, shape, dt=F32):
            return [sb(f"{name}{i}", shape, dt) for i in range(n)]
        return sb, sbn

    sbg, sbng = mk_sb(es_g)

    eye = sbg("eye", [128, 128])
    eyeb = sbg("eyeb", [128, 128], BF16)
    gacol = sbg("gacol", [128, 16])
    gbcol = sbg("gbcol", [128, 16])
    statc = sbng("statc", 6, [128, 1])
    tq = tqb = tqbs = stg = xst = tqnW = tqbnW = None

    sems = {e: es_g.enter_context(nc.semaphore(f"sem_{e}")) for e in ENGINES}
    dsems = {e: [es_g.enter_context(nc.semaphore(f"dsem_{e}_{s}")) for s in range(NSLOT)]
             for e in ("sync", "gpsimd", "scalar", "vector")}
    dsems["tensor"] = []
    blk = es_g.enter_context(nc.Block())

    def dma(eng, dst, src, is_out=False, slow=False):
        db, dap = _ap(dst)
        sbb, sap = _ap(src)
        if slow:
            return pb.emit(eng, lambda EE, dap=dap, sap=sap: EE.dma_start(
                out=dap, in_=sap, allow_slow_non_contiguous=True),
                reads=(sbb,), writes=(db,), dma=True, is_out=is_out)
        return pb.emit(eng, lambda EE, dap=dap, sap=sap: EE.dma_start(out=dap, in_=sap),
                       reads=(sbb,), writes=(db,), dma=True, is_out=is_out)

    def s_act(out, in_, func, bias=None, scale=None):
        ob, oap = _ap(out)
        ib, iap = _ap(in_)
        reads = (ib,)
        kw = {}
        if bias is not None:
            if isinstance(bias, tuple):
                kw["bias"] = bias[1]
                reads += (bias[0],)
            else:
                kw["bias"] = bias
        if scale is not None:
            if isinstance(scale, tuple):
                kw["scale"] = scale[1]
                reads += (scale[0],)
            else:
                kw["scale"] = scale
        return pb.emit("scalar",
                       lambda EE, oap=oap, iap=iap, kw=kw: EE.activation(out=oap, in_=iap, func=func, **kw),
                       reads=reads, writes=(ob,))

    def s_copy(out, in_):
        ob, oap = _ap(out)
        ib, iap = _ap(in_)
        return pb.emit("scalar", lambda EE, oap=oap, iap=iap: EE.copy(out=oap, in_=iap),
                       reads=(ib,), writes=(ob,))

    def t_mm(out, lhsT, rhs, start, stop):
        ob, oap = _ap(out)
        lb, lap = _ap(lhsT)
        rb, rap = _ap(rhs)
        return pb.emit("tensor",
                       lambda EE, oap=oap, lap=lap, rap=rap, st=start, sp=stop:
                       EE.matmul(out=oap, lhsT=lap, rhs=rap, start=st, stop=sp),
                       reads=(lb, rb), writes=(ob,))

    def t_tr(out, in_, bf=False, ident=None):
        ob, oap = _ap(out)
        ib, iap = _ap(in_)
        idb = ident if ident is not None else (eyeb if bf else eye)
        return pb.emit("tensor",
                       lambda EE, oap=oap, iap=iap, idap=idb.ap:
                       EE.transpose(out=oap, in_=iap, identity=idap),
                       reads=(ib, idb), writes=(ob,))

    def v_tt(out, a, b, op, eng="vector"):
        ob, oap = _ap(out)
        ab, aap = _ap(a)
        bb, bap = _ap(b)
        return pb.emit(eng,
                       lambda EE, oap=oap, aap=aap, bap=bap, op=op:
                       EE.tensor_tensor(out=oap, in0=aap, in1=bap, op=op),
                       reads=(ab, bb), writes=(ob,))

    def v_ts(out, a, s1, op0, s2=None, op1=None, eng="vector"):
        ob, oap = _ap(out)
        ab, aap = _ap(a)
        reads = (ab,)
        if isinstance(s1, tuple):
            reads += (s1[0],)
            s1 = s1[1]
        if isinstance(s2, tuple):
            reads += (s2[0],)
            s2 = s2[1]
        if op1 is None:
            return pb.emit(eng,
                           lambda EE, oap=oap, aap=aap, s1=s1, op0=op0: EE.tensor_scalar(
                               out=oap, in0=aap, scalar1=s1, scalar2=None, op0=op0),
                           reads=reads, writes=(ob,))
        return pb.emit(eng,
                       lambda EE, oap=oap, aap=aap, s1=s1, s2=s2, op0=op0, op1=op1:
                       EE.tensor_scalar(out=oap, in0=aap, scalar1=s1, scalar2=s2,
                                        op0=op0, op1=op1),
                       reads=reads, writes=(ob,))

    def v_stt(out, in0, scalar, in1, op0, op1):
        ob, oap = _ap(out)
        a0, a0ap = _ap(in0)
        a1, a1ap = _ap(in1)
        reads = (a0, a1)
        if isinstance(scalar, tuple):
            reads += (scalar[0],)
            scalar = scalar[1]
        return pb.emit("vector",
                       lambda EE, oap=oap, a0ap=a0ap, a1ap=a1ap, s=scalar, op0=op0, op1=op1:
                       EE.scalar_tensor_tensor(out=oap, in0=a0ap, scalar=s, in1=a1ap,
                                               op0=op0, op1=op1),
                       reads=reads, writes=(ob,))

    def v_copy(out, in_, eng="vector"):
        ob, oap = _ap(out)
        ib, iap = _ap(in_)
        return pb.emit(eng, lambda EE, oap=oap, iap=iap: EE.tensor_copy(out=oap, in_=iap),
                       reads=(ib,), writes=(ob,))

    fired = set()

    def tap2(name, src):
        if name in DEBUG_TAPS and name not in fired:
            fired.add(name)
            tap(name, src)

    def dma_tr(dst, src, eng="sync"):
        db, dap = _ap(dst)
        sbb, sap = _ap(src)
        return pb.emit(eng, lambda EE, dap=dap, sap=sap: EE.dma_start(out=dap, in_=sap,
                                                                      transpose=True),
                       reads=(sbb,), writes=(db,), dma=True)

    def tap(name, src):
        if name in DEBUG_TAPS:
            sbb, sap = _ap(src)
            shape = list(sap.shape)
            t = hout("tap_" + name, shape, sap.dtype)
            taps[name] = t
            dma("gpsimd", t, (sbb, sap), is_out=True)

    from functools import partial
    qrr = [0]
    qpend = []

    def _emit_rr(chains):
        chains = [list(c) for c in chains if c]
        while chains:
            for c in list(chains):
                c.pop(0)()
                if not c:
                    chains.remove(c)

    def quantize_flush():
        _emit_rr(qpend)
        qpend.clear()

    def quantize_multi(jobs, W, mfrac, mmax, posts=None):
        nblk = W // 32
        J = len(jobs)
        par = qrr[0] % 2
        qrr[0] += 1
        T = J * nblk
        tW = tqnW[par]
        tS = tqbnW[par]

        def wa(k):
            return (tW, tW.ap[:, k * 64:k * 64 + T])
        m, e_, u0, tk, fl, rb = (wa(0), wa(1), wa(2), wa(3), wa(4), wa(5))
        sbw = (tS, tS.ap[:, 0:T])
        blk = []
        bs = blk.append
        for j, (vt, _o) in enumerate(jobs):
            vb, vap = _ap(vt)
            bs(partial(pb.emit, "vector",
                       lambda EE, oap2=tW.ap[:, j * nblk:j * nblk + nblk].unsqueeze(2),
                       iap=vap.rearrange("p (b c) -> p b c", c=32):
                       EE.tensor_reduce(out=oap2, in_=iap, axis=AXX, op=OP.max,
                                        apply_absolute_value=True),
                       reads=(vb,), writes=(tW,)))
        bs(partial(v_ts, m, m, 1e-30, OP.max))
        bs(partial(s_act, e_, m, AF.Ln))
        bs(partial(v_ts, e_, e_, 1.0 / LN2, OP.mult, 32.0, OP.add))
        bs(partial(v_copy, u0, e_))
        for k in (32.0, 16.0, 8.0, 4.0, 2.0, 1.0):
            bs(partial(v_ts, tk, u0, k, OP.is_ge, k, OP.mult))
            bs(partial(v_tt, u0, u0, tk, OP.subtract))
        bs(partial(v_tt, fl, e_, u0, OP.subtract))
        bs(partial(v_ts, tk, fl, -LN2, OP.mult, 34.0 * LN2, OP.add))
        bs(partial(s_act, rb, tk, AF.Exp))
        bs(partial(v_ts, tk, fl, LN2, OP.mult, -34.0 * LN2, OP.add))
        bs(partial(s_act, fl, tk, AF.Exp))
        bs(partial(v_copy, sbw, fl))

        delta = 2.0 ** -(mfrac + 1)
        lv = [4.0, 2.0, 1.0] + [2.0 ** -(jj + 1) for jj in range(mfrac)]
        chains = []
        for j, (vt, out_bf) in enumerate(jobs):
            vb, vap = _ap(vt)
            ob, oap = _ap(out_bf)
            rbj = (tW, tW.ap[:, 5 * 64 + j * nblk:5 * 64 + (j + 1) * nblk])
            sbj = (tS, tS.ap[:, j * nblk:(j + 1) * nblk])
            tb = par * 8 + j * 4
            A = (tq[tb + 0], tq[tb + 0].ap[:, 0:W])
            B = (tq[tb + 1], tq[tb + 1].ap[:, 0:W])
            C = (tq[tb + 2], tq[tb + 2].ap[:, 0:W])
            Dd = (tq[tb + 3], tq[tb + 3].ap[:, 0:W])
            st = []
            a = st.append
            a(partial(pb.emit, "vector",
                      lambda EE, oap2=A[1].rearrange("p (b c) -> p b c", c=32),
                      iap=vap.rearrange("p (b c) -> p b c", c=32),
                      rap=rbj[1].unsqueeze(2).broadcast_to([128, nblk, 32]):
                      EE.tensor_tensor(out=oap2, in0=iap, in1=rap, op=OP.mult),
                      reads=(vb, tW), writes=(A[0],)))
            a(partial(s_act, B, A, AF.Abs))
            a(partial(v_ts, C, B, 2.0, OP.is_ge))
            a(partial(v_ts, B, B, 4.0, OP.is_ge))
            a(partial(v_ts, C, C, -0.5, OP.mult, 1.0, OP.add))
            a(partial(v_stt, B, B, -0.25, C, OP.mult, OP.add))
            a(partial(pb.emit, "vector", lambda EE, oap3=C[1], iap=B[1]:
                      EE.reciprocal(out=oap3, in_=iap),
                      reads=(B[0],), writes=(C[0],)))
            a(partial(v_tt, B, A, B, OP.mult))
            a(partial(v_ts, A, B, 2.0, OP.add))
            for k in lv:
                a(partial(v_ts, Dd, A, k - delta, OP.is_ge, k, OP.mult))
                a(partial(v_tt, A, A, Dd, OP.subtract))
            a(partial(v_tt, B, B, A, OP.subtract))
            a(partial(v_tt, B, B, C, OP.mult))
            a(partial(v_ts, B, B, float(mmax), OP.min, float(-mmax), OP.max))
            a(partial(pb.emit, "vector",
                      lambda EE, oap2=oap.rearrange("p (b c) -> p b c", c=32),
                      iap=B[1].rearrange("p (b c) -> p b c", c=32),
                      sap=sbj[1].unsqueeze(2).broadcast_to([128, nblk, 32]):
                      EE.tensor_tensor(out=oap2, in0=iap, in1=sap, op=OP.mult),
                      reads=(B[0], tS), writes=(ob,)))
            if posts:
                st.extend(posts[j])
            chains.append(st)
        prev = qpend[:]
        qpend.clear()
        _emit_rr(prev + [blk])
        qpend.extend(chains)

    es_a = contextlib.ExitStack()
    sba, sbna = mk_sb(es_a)
    tq = sbna("tqA", 3, [128, 1024])
    stg = sbna("stgA", 2, [128, 512])
    xst = sbna("xstA", 2, [128, 2048])
    maskinf = sba("maskinf", [128, 128])
    ones128 = sba("ones128", [128, 128])
    eye16 = sba("eye16", [16, 16])
    onescol = sba("onescol", [128, 1])
    nwcol = sba("nwcol", [128, 16])
    ccol = sba("ccol", [128, 16])
    scT = sba("scT", [128, 16])
    diag16 = sba("diag16", [16, 16])
    sstate = sbna("sstate", 16, [128, 128])
    ldT = sba("ldT", [16, QT])
    cumT = sba("cumT", [16, QT])
    zeros16 = sba("zeros16", [16, 128])
    cumcol = sbna("cumcol", 4, [128, 16])
    ekcol = sbna("ekcol", 4, [128, 16])
    etot = sbna("etot", 4, [128, 16])
    ecol = sbna("ecol", 4, [128, 16])
    bc1 = sba("bc1", [128, 2048])
    bc2 = sba("bc2", [128, 2048])
    hT = sbna("hT", 16, [128, QT])
    oT = sbna("oT", 16, [128, QT])
    wstripe = sbna("wstripe", 2, [128, 2048])
    qkv = sbna("qkv", 3, [128, QT])
    att = sbna("att", 12, [128, 128])
    ktok = sbna("ktok", 2, [128, 128])
    vtok = sbna("vtok", 2, [128, 128])
    wmt = sbna("wmt", 2, [128, 512])
    wfb = sba("wfb", [128, 256])
    srow = sbna("srow", 2, [1, 512])

    ps = [Buf(es_a.enter_context(nc.psum_tensor(f"ps{i}", [128, 512], F32)).ap(), f"ps{i}")
          for i in range(8)]

    for dst, src in ((eye, eye_h), (eyeb, eyeb_h), (maskinf, mask_h), (ones128, ones128_h),
                     (eye16, eye16_h), (onescol, onescol_h), (nwcol, nw_h), (ccol, c_h),
                     (wfb, wf_h)):
        dma("sync", dst, src)
    pb.emit("gpsimd", lambda EE: EE.memset(zeros16.ap, 0.0), reads=(), writes=(zeros16,))
    for h in range(16):
        pb.emit("gpsimd", lambda EE, a=sstate[h].ap: EE.memset(a, 0.0),
                reads=(), writes=(sstate[h],))

    s_act(scT, ccol, AF.Silu)
    for jt in range(24):
        pbank = ps[jt % 2]
        for dc in range(16):
            wb = wmt[dc % 2]
            dma("sync", wb, (wmod_h, wmod_h.ap[dc * 128:(dc + 1) * 128, jt * 512:(jt + 1) * 512]))
            t_mm((pbank, pbank.ap[0:1, :]), (scT, scT.ap[:, dc:dc + 1]), wb,
                 start=(dc == 0), stop=(dc == 15))
        sr = (srow[jt % 2], srow[jt % 2].ap)
        s_copy(sr, (pbank, pbank.ap[0:1, :]))
        dma("gpsimd", (mod_s, mod_s.ap[:, jt * 512:(jt + 1) * 512]), sr)
    tap("modrow", (Buf(mod_s.ap, "m"), mod_s.ap))

    for col_out, idx in ((gacol, 2), (gbcol, 5)):
        raw = (tq[0], tq[0].ap[:, 0:16])
        dma("sync", raw, (mod_s, mod_s.ap[0:1, idx * 2048:(idx + 1) * 2048]
                          .rearrange("o (b c) -> (o c) b", c=128)), slow=True)
        t0 = (tq[1], tq[1].ap[:, 0:16])
        t1 = (tq[2], tq[2].ap[:, 0:16])
        v_ts(t0, raw, 0.0, OP.is_ge, 2.0, OP.mult)
        v_ts(t0, t0, -1.0, OP.add)
        v_ts(t1, raw, -1.0, OP.mult)
        v_tt(t1, raw, t1, OP.max)
        v_ts(t1, t1, 0.25, OP.max)
        v_tt(col_out, t0, t1, OP.mult)

    def bcast_mod(dst, idx, add1):
        src = mod_s.ap[0:1, idx * 2048:(idx + 1) * 2048].partition_broadcast(128).squeeze(1)
        dma("sync", dst, (mod_s, src))
        if add1:
            v_ts(dst, dst, 1.0, OP.add)

    bcast_mod(bc1, 1, True)
    bcast_mod(bc2, 0, False)
    tap("bc1", bc1)

    def ln1_q(qt):
        for icl in range(4):
            ic = qt * 4 + icl
            xt = xst[icl % 2]
            dma("sync", xt, (x_h, x_h.ap[ic * 128:(ic + 1) * 128, :]))
            s1 = (statc[0], statc[0].ap)
            s2 = (statc[1], statc[1].ap)
            for pp in range(2):
                xsl = (xt, xt.ap[:, pp * 1024:(pp + 1) * 1024])
                sq = (tq[0], tq[0].ap)
                s_act(sq, xsl, AF.Square)
                ps1 = (statc[2 + pp], statc[2 + pp].ap)
                pb.emit("vector", lambda EE, oap=ps1[1].unsqueeze(2), iap=xsl[1].unsqueeze(1):
                        EE.tensor_reduce(out=oap, in_=iap, axis=AXX, op=OP.add),
                        reads=(xt,), writes=(ps1[0],))
                ps2 = (statc[4 + pp], statc[4 + pp].ap)
                pb.emit("vector", lambda EE, oap=ps2[1].unsqueeze(2), iap=sq[1].unsqueeze(1):
                        EE.tensor_reduce(out=oap, in_=iap, axis=AXX, op=OP.add),
                        reads=(sq[0],), writes=(ps2[0],))
            v_tt(s1, (statc[2], statc[2].ap), (statc[3], statc[3].ap), OP.add)
            v_tt(s2, (statc[4], statc[4].ap), (statc[5], statc[5].ap), OP.add)
            mu = (statc[2], statc[2].ap)
            v_ts(mu, s1, 1.0 / D, OP.mult)
            mu2 = (statc[3], statc[3].ap)
            v_tt(mu2, mu, mu, OP.mult)
            var = (statc[4], statc[4].ap)
            v_ts(s2, s2, 1.0 / D, OP.mult)
            v_tt(var, s2, mu2, OP.subtract)
            v_ts(var, var, 1e-5, OP.add)
            rin = (statc[5], statc[5].ap)
            pb.emit("vector", lambda EE, oap=rin[1], iap=var[1]:
                    EE.reciprocal(out=oap, in_=iap), reads=(var[0],), writes=(rin[0],))
            tap2("dbg_mu2", statc[3])
            tap2("dbg_var", statc[4])
            tap2("dbg_rin", statc[5])
            s_act(rin, rin, AF.Sqrt)
            for pp in range(2):
                xsl = (xt, xt.ap[:, pp * 1024:(pp + 1) * 1024])
                xh = (tq[0], tq[0].ap)
                v_ts(xh, xsl, (statc[2], statc[2].ap), OP.subtract,
                     (statc[5], statc[5].ap), OP.mult)
                hh = (tq[1], tq[1].ap)
                v_tt(hh, xh, (bc1, bc1.ap[:, pp * 1024:(pp + 1) * 1024]), OP.mult)
                v_tt(hh, hh, (bc2, bc2.ap[:, pp * 1024:(pp + 1) * 1024]), OP.add)
                for f8 in range(8):
                    fc = pp * 8 + f8
                    pbank = ps[4 + (f8 % 2)]
                    pt = (pbank, pbank.ap[:, 0:128])
                    t_tr(pt, (hh[0], hh[1][:, f8 * 128:(f8 + 1) * 128]))
                    s_copy((hT[fc], hT[fc].ap[:, icl * 128:(icl + 1) * 128]), pt)

    def ld_q(qt):
        pbank = ps[6]
        for fc in range(16):
            t_mm((pbank, pbank.ap[0:16, :]),
                 (wfb, wfb.ap[:, fc * 16:(fc + 1) * 16]),
                 hT[fc], start=(fc == 0), stop=(fc == 15))
        s_act(ldT, (pbank, pbank.ap[0:16, :]), AF.Sigmoid)
        s_act(ldT, ldT, AF.Ln)
        for mc in range(4):
            pb.emit("vector",
                    lambda EE, oap=cumT.ap[0:16, mc * 128:(mc + 1) * 128],
                    iap=ldT.ap[0:16, mc * 128:(mc + 1) * 128], zap=zeros16.ap:
                    EE.tensor_tensor_scan(out=oap, data0=iap, data1=zap, initial=0.0,
                                          op0=OP.add, op1=OP.add),
                    reads=(ldT, zeros16), writes=(cumT,))
            pt = (ps[6], ps[6].ap[:, 0:16])
            t_tr(pt, (cumT, cumT.ap[0:16, mc * 128:(mc + 1) * 128]), ident=eye16)
            v_copy(cumcol[mc], pt)
            s_act(ecol[mc], cumcol[mc], AF.Exp)
            dg = (diag16, diag16.ap)
            pb.emit("vector",
                    lambda EE, oap=dg[1], iap=eye16.ap,
                    sap=cumT.ap[0:16, mc * 128 + 127:mc * 128 + 128]:
                    EE.tensor_scalar(out=oap, in0=iap, scalar1=sap, scalar2=None, op0=OP.mult),
                    reads=(eye16, cumT), writes=(diag16,))
            pt2 = (ps[6], ps[6].ap[:, 16:32])
            t_mm(pt2, (ones128, ones128.ap[0:16, 0:128]), dg, start=True, stop=True)
            s_act(etot[mc], pt2, AF.Exp)
            v_tt(ekcol[mc], pt2, cumcol[mc], OP.subtract)
            s_act(ekcol[mc], ekcol[mc], AF.Exp)

    def attn_q(qt):
        for h in range(16):
            for wi, (wh, dst, do_silu) in enumerate(((wq_h, qkv[0], True),
                                                     (wk_h, qkv[1], True),
                                                     (wv_h, qkv[2], False))):
                wsb = wstripe[wi % 2]
                dma("sync", wsb, (wh, wh.ap[h * 128:(h + 1) * 128, :]))
                pbank = ps[wi % 2]
                for fc in range(16):
                    t_mm(pbank, (wsb, wsb.ap[:, fc * 128:(fc + 1) * 128]),
                         hT[fc], start=(fc == 0), stop=(fc == 15))
                if do_silu:
                    s_act(dst, pbank, AF.Silu)
                else:
                    s_copy(dst, pbank)
            for mc in range(4):
                par = mc % 2
                qs_ = (qkv[0], qkv[0].ap[:, mc * 128:(mc + 1) * 128])
                ks_ = (qkv[1], qkv[1].ap[:, mc * 128:(mc + 1) * 128])
                vs_ = (qkv[2], qkv[2].ap[:, mc * 128:(mc + 1) * 128])
                pbank = ps[2]
                t_tr((pbank, pbank.ap[:, 0:128]), ks_)
                s_copy(ktok[par], (pbank, pbank.ap[:, 0:128]))
                t_tr((pbank, pbank.ap[:, 128:256]), vs_)
                s_copy(vtok[par], (pbank, pbank.ap[:, 128:256]))
                cc = att[par * 6 + 0]
                v_ts(cc, ones128, (cumcol[mc], cumcol[mc].ap[:, h:h + 1]), OP.mult)
                pA = (ps[3], ps[3].ap[:, 0:128])
                t_tr(pA, cc)
                ec = att[par * 6 + 1]
                v_ts(ec, ones128, (ecol[mc], ecol[mc].ap[:, h:h + 1]), OP.mult)
                pE = (ps[3], ps[3].ap[:, 128:256])
                t_tr(pE, ec)
                dd = att[par * 6 + 2]
                v_ts(dd, pA, (cumcol[mc], cumcol[mc].ap[:, h:h + 1]), OP.subtract)
                v_tt(dd, dd, maskinf, OP.add)
                s_act(dd, dd, AF.Exp)
                pS = (ps[4], ps[4].ap[:, 0:128])
                t_mm(pS, ks_, qs_, start=True, stop=True)
                st_ = att[par * 6 + 3]
                v_tt(st_, pS, dd, OP.mult)
                qd = att[par * 6 + 4]
                v_tt(qd, qs_, pE, OP.mult)
                pO = (ps[5], ps[5].ap[:, 0:128])
                t_mm(pO, vtok[par], st_, start=True, stop=False)
                t_mm(pO, sstate[h], qd, start=False, stop=True)
                s_copy((oT[h], oT[h].ap[:, mc * 128:(mc + 1) * 128]), pO)
                kd = att[par * 6 + 5]
                v_ts(kd, ktok[par], (ekcol[mc], ekcol[mc].ap[:, h:h + 1]), OP.mult)
                pST = (ps[7], ps[7].ap[:, 0:128])
                t_mm(pST, kd, vtok[par], start=True, stop=True)
                v_stt(sstate[h], sstate[h], (etot[mc], etot[mc].ap[:, h:h + 1]),
                      pST, OP.mult, OP.add)

    def rms_q(qt):
        pR = (ps[6], ps[6].ap[0:1, :])
        for h in range(16):
            sq_ = (tq[0], tq[0].ap[:, 0:512])
            s_act(sq_, oT[h], AF.Square)
            t_mm(pR, onescol, sq_, start=(h == 0), stop=(h == 15))
        rrow = (srow[0], srow[0].ap)
        v_ts(rrow, pR, 1.0 / D, OP.mult, 1e-6, OP.add)
        pb.emit("vector", lambda EE, oap=rrow[1], iap=rrow[1]:
                EE.reciprocal(out=oap, in_=iap), reads=(srow[0],), writes=(srow[0],))
        s_act(rrow, rrow, AF.Sqrt)
        pRB = ps[3]
        t_mm(pRB, (ones128, ones128.ap[0:1, :]), rrow, start=True, stop=True)
        for h in range(16):
            v_stt(oT[h], oT[h], (nwcol, nwcol.ap[:, h:h + 1]), pRB, OP.mult, OP.mult)

    def wo_q(qt):
        for oc in range(16):
            wsb = wstripe[oc % 2]
            dma("sync", wsb, (wo_h, wo_h.ap[oc * 128:(oc + 1) * 128, :]))
            pbank = ps[oc % 2]
            for fc in range(16):
                t_mm(pbank, (wsb, wsb.ap[:, fc * 128:(fc + 1) * 128]),
                     oT[fc], start=(fc == 0), stop=(fc == 15))
            stg_ = stg[oc % 2]
            s_act(stg_, pbank, AF.Copy, scale=(gacol, gacol.ap[:, oc:oc + 1]))
            for t4 in range(4):
                pt = (ps[2], ps[2].ap[:, (t4 % 2) * 128:(t4 % 2) * 128 + 128])
                t_tr(pt, (stg_, stg_.ap[:, t4 * 128:(t4 + 1) * 128]))
                ic = qt * 4 + t4
                stg2 = (att[6 + t4 % 2], att[6 + t4 % 2].ap)
                v_copy(stg2, pt)
                dma("gpsimd", (pat_s, pat_s.ap[ic * 128:(ic + 1) * 128,
                                               oc * 128:(oc + 1) * 128]), stg2)

    stage = [0]

    def go(f, *a):
        stage[0] += 1
        if STOP_AFTER is not None and stage[0] > STOP_AFTER:
            return
        f(*a)

    for qt in range(NQ):
        go(ln1_q, qt)
        if qt == 0:
            tap("bc1x", bc1)
            tap("hT0", hT[0])
            tap("hT15", hT[15])
            tap("dbg_s1", statc[0])
            tap("dbg_s2", statc[1])
            tap("dbg_mu", statc[2])
            tap("dbg_r", statc[5])
            tap("dbg_hh", tq[1])
        go(ld_q, qt)
        if qt == 0:
            tap("cumT0", cumT)
            tap("ekcol0", ekcol[3])
            tap("etot0", etot[3])
        go(attn_q, qt)
        if qt == 0:
            tap("qT0", qkv[0])
            tap("kT0", qkv[1])
            tap("vT0", qkv[2])
            tap("oTraw0", oT[0])
            tap("state0", sstate[0])
        go(rms_q, qt)
        if qt == 0:
            tap("oTn0", oT[0])
        go(wo_q, qt)
    es_a.close()
    pb.fence()

    es_b = contextlib.ExitStack()
    sbb_, sbnb = mk_sb(es_b)
    tq = sbnb("tqB", 16, [128, 1024])
    tqnW = [sbb_("tqnWB0", [128, 384]), sbb_("tqnWB1", [128, 384])]
    tqbnW = [sbb_("tqbnWB0", [128, 128], BF16), sbb_("tqbnWB1", [128, 128], BF16)]
    tqb = [sbb_("tqb0B", [128, 2048], BF16), sbb_("tqb1B", [128, 2048], BF16)]
    tqbs = sbb_("tqbsB", [128, 1024], BF16)

    psB = [Buf(es_b.enter_context(nc.psum_tensor(f"psB{i}", [128, 512], F32)).ap(), f"psB{i}")
           for i in range(6)]
    psBb = Buf(es_b.enter_context(nc.psum_tensor("psBb", [128, 512], BF16)).ap(), "psBb")
    qaT = sbnb("qaT", 16, [128, N], BF16)
    es_b1 = contextlib.ExitStack()
    sbb1, sbnb1 = mk_sb(es_b1)
    xst = sbnb1("xstB", 2, [128, 2048])
    bc1B = sbb1("bc1B", [128, 2048])
    bc2B = sbb1("bc2B", [128, 2048])
    x1t = sbb1("x1t", [128, 2048])

    def p5():
        bcast_mod(bc1B, 4, True)
        bcast_mod(bc2B, 3, False)
        for ic in range(16):
            xt = xst[ic % 2]
            dma("sync", xt, (x_h, x_h.ap[ic * 128:(ic + 1) * 128, :]))
            pc = (tq[6], tq[6].ap)
            x1 = (x1t, x1t.ap)
            for pp in range(2):
                pcs = (tq[6], tq[6].ap[:, 0:1024])
                dma("sync", pcs, (pat_s, pat_s.ap[ic * 128:(ic + 1) * 128,
                                                  pp * 1024:(pp + 1) * 1024]))
                v_tt((x1t, x1t.ap[:, pp * 1024:(pp + 1) * 1024]),
                     (xt, xt.ap[:, pp * 1024:(pp + 1) * 1024]), pcs, OP.add)
            dma("gpsimd", (x1_s, x1_s.ap[ic * 128:(ic + 1) * 128, :]), x1)
            s1 = (statc[0], statc[0].ap)
            s2 = (statc[1], statc[1].ap)
            for pp in range(2):
                xsl = (x1t, x1t.ap[:, pp * 1024:(pp + 1) * 1024])
                sq = (tq[0], tq[0].ap)
                s_act(sq, xsl, AF.Square)
                ps1 = (statc[2 + pp], statc[2 + pp].ap)
                pb.emit("vector", lambda EE, oap=ps1[1].unsqueeze(2), iap=xsl[1].unsqueeze(1):
                        EE.tensor_reduce(out=oap, in_=iap, axis=AXX, op=OP.add),
                        reads=(x1t,), writes=(ps1[0],))
                ps2 = (statc[4 + pp], statc[4 + pp].ap)
                pb.emit("vector", lambda EE, oap=ps2[1].unsqueeze(2), iap=sq[1].unsqueeze(1):
                        EE.tensor_reduce(out=oap, in_=iap, axis=AXX, op=OP.add),
                        reads=(sq[0],), writes=(ps2[0],))
            v_tt(s1, (statc[2], statc[2].ap), (statc[3], statc[3].ap), OP.add)
            v_tt(s2, (statc[4], statc[4].ap), (statc[5], statc[5].ap), OP.add)
            v_ts((statc[2], statc[2].ap), s1, 1.0 / D, OP.mult)
            v_tt((statc[3], statc[3].ap), (statc[2], statc[2].ap),
                 (statc[2], statc[2].ap), OP.mult)
            v_ts(s2, s2, 1.0 / D, OP.mult)
            v_tt((statc[4], statc[4].ap), s2, (statc[3], statc[3].ap), OP.subtract)
            v_ts((statc[4], statc[4].ap), (statc[4], statc[4].ap), 1e-5, OP.add)
            pb.emit("vector", lambda EE, oap=statc[5].ap, iap=statc[4].ap:
                    EE.reciprocal(out=oap, in_=iap), reads=(statc[4],), writes=(statc[5],))
            s_act((statc[5], statc[5].ap), (statc[5], statc[5].ap), AF.Sqrt)
            jobs = []
            posts = []
            h2b = xst[ic % 2]
            qb_t = tqb[qrr[0] % 2]
            for pp in range(2):
                xsl = (x1t, x1t.ap[:, pp * 1024:(pp + 1) * 1024])
                h2 = (h2b, h2b.ap[:, pp * 1024:(pp + 1) * 1024])
                v_ts(h2, xsl, (statc[2], statc[2].ap), OP.subtract,
                     (statc[5], statc[5].ap), OP.mult)
                v_tt(h2, h2, (bc1B, bc1B.ap[:, pp * 1024:(pp + 1) * 1024]), OP.mult)
                v_tt(h2, h2, (bc2B, bc2B.ap[:, pp * 1024:(pp + 1) * 1024]), OP.add)
                jobs.append((h2, (qb_t, qb_t.ap[:, pp * 1024:(pp + 1) * 1024])))
                posts.append([partial(dma_tr,
                                      (qaT[pp * 8 + f8],
                                       qaT[pp * 8 + f8].ap[:, ic * 128:(ic + 1) * 128]),
                                      (qb_t, qb_t.ap[:, pp * 1024 + f8 * 128:
                                                     pp * 1024 + (f8 + 1) * 128]))
                              for f8 in range(8)])
            quantize_multi(jobs, 1024, 3, 7.5, posts)
        quantize_flush()

    def p6a():
        for obg in range(16):
            for phase, (wsrc, is_gate) in enumerate(((wg_h, True), (wu_h, False))):
                qg = qgT[0]
                for r4 in range(4):
                    wt = wgrow[r4 % 2]
                    wtv = (wt, wt.ap)
                    dma("sync", wtv, (wsrc, wsrc.ap[(obg * 4 + r4) * 128:
                                                    (obg * 4 + r4 + 1) * 128, :]))
                    qb_t = tqb[qrr[0] % 2]
                    jobs = []
                    posts = []
                    for pp in range(2):
                        jobs.append(((wt, wt.ap[:, pp * 1024:(pp + 1) * 1024]),
                                     (qb_t, qb_t.ap[:, pp * 1024:(pp + 1) * 1024])))
                        posts.append([partial(dma_tr,
                                              (qg[pp * 8 + f8],
                                               qg[pp * 8 + f8].ap[:, r4 * 128:(r4 + 1) * 128]),
                                              (qb_t, qb_t.ap[:, pp * 1024 + f8 * 128:
                                                             pp * 1024 + (f8 + 1) * 128]))
                                      for f8 in range(8)])
                    quantize_multi(jobs, 1024, 1, 6.0, posts)
                quantize_flush()
                for icq in range(4):
                    for i4 in range(4):
                        ic = icq * 4 + i4
                        for fc in range(16):
                            t_mm(psB[i4], (qaT[fc], qaT[fc].ap[:, ic * 128:(ic + 1) * 128]),
                                 qg[fc], start=(fc == 0), stop=(fc == 15))
                        if is_gate:
                            s_act(gstripe[ic], psB[i4], AF.Silu)
                        else:
                            v_tt(gstripe[ic], psB[i4], gstripe[ic], OP.mult)
            for icp in range(8):
                qb_t = tqb[qrr[0] % 2]
                jobs = []
                posts = []
                for jj in range(2):
                    ic = icp * 2 + jj
                    jobs.append(((gstripe[ic], gstripe[ic].ap),
                                 (qb_t, qb_t.ap[:, jj * 1024:jj * 1024 + 512])))
                    pl = []
                    for t4 in range(4):
                        sl = (jj * 4 + t4) * 128
                        stg2 = (tqbs, tqbs.ap[:, sl:sl + 128])
                        pl.append(partial(dma_tr, stg2,
                                          (qb_t, qb_t.ap[:, jj * 1024 + t4 * 128:
                                                         jj * 1024 + (t4 + 1) * 128])))
                        irow = (obg * 4 + t4) * 128
                        pl.append(partial(dma, "gpsimd",
                                          (qint_s, qint_s.ap[irow:irow + 128,
                                                             ic * 128:(ic + 1) * 128]),
                                          stg2))
                    posts.append(pl)
                quantize_multi(jobs, 512, 3, 7.5, posts)
        quantize_flush()

    go(p5)
    tap("x1t0", x1t)
    tap("qaT0", qaT[0])
    tap("qaT15", qaT[15])
    es_b1.close()
    pb.fence()
    es_b2 = contextlib.ExitStack()
    sbb2, sbnb2 = mk_sb(es_b2)
    gstripe = sbnb2("gstripe", 16, [128, 512])
    qgT = [sbnb2("qgT", 16, [128, 512], BF16)]
    wgrow = sbnb2("wgrow", 2, [128, 2048])
    go(p6a)
    es_b2.close()
    tap("gs0", gstripe[0])
    tap("qgT0", qgT[0])
    es_b.close()
    pb.fence()

    es_c = contextlib.ExitStack()
    sbc, sbnc = mk_sb(es_c)
    tq = sbnc("tqC", 16, [128, 1024])
    tqnW = [sbc("tqnWC0", [128, 384]), sbc("tqnWC1", [128, 384])]
    tqbnW = [sbc("tqbnWC0", [128, 128], BF16), sbc("tqbnWC1", [128, 128], BF16)]
    tqb = [sbc("tqb0C", [128, 2048], BF16), sbc("tqb1C", [128, 2048], BF16)]
    tqbs = sbc("tqbsC", [128, 1024], BF16)
    psCb = Buf(es_c.enter_context(nc.psum_tensor("psCb", [128, 512], BF16)).ap(), "psCb")
    wdrow = sbnc("wdrow", 2, [128, 8192])

    def p6w():
        for rt in range(16):
            wdr = wdrow[rt % 2]
            dma("sync", wdr, (wd_h, wd_h.ap[rt * 128:(rt + 1) * 128, :]))
            for pcp in range(4):
                qb_t = tqb[qrr[0] % 2]
                jobs = []
                posts = []
                for jj in range(2):
                    piece = pcp * 2 + jj
                    jobs.append(((wdr, wdr.ap[:, piece * 1024:(piece + 1) * 1024]),
                                 (qb_t, qb_t.ap[:, jj * 1024:(jj + 1) * 1024])))
                    pl = []
                    for t8 in range(8):
                        Ic = piece * 8 + t8
                        sl = ((t8 * 2 + jj) % 8) * 128
                        stg2 = (tqbs, tqbs.ap[:, sl:sl + 128])
                        pl.append(partial(dma_tr, stg2,
                                          (qb_t, qb_t.ap[:, jj * 1024 + t8 * 128:
                                                         jj * 1024 + (t8 + 1) * 128])))
                        pl.append(partial(dma, "gpsimd",
                                          (wdq_s, wdq_s.ap[rt * 128:(rt + 1) * 128,
                                                           Ic * 128:(Ic + 1) * 128]),
                                          stg2))
                    posts.append(pl)
                quantize_multi(jobs, 1024, 1, 6.0, posts)
        quantize_flush()

    go(p6w)
    es_c.close()
    pb.fence()

    es_d = contextlib.ExitStack()
    sbd, sbnd = mk_sb(es_d)
    tq = sbnd("tqD", 2, [128, 1024])
    stg = sbnd("stgD", 2, [128, 512])
    xst = sbnd("xstD", 2, [128, 2048])
    psD = [Buf(es_d.enter_context(nc.psum_tensor(f"psD{i}", [128, 512], F32)).ap(), f"psD{i}")
           for i in range(6)]
    qiT = sbnd("qiT", 64, [128, 1024], BF16)
    wdq_sb = sbnd("wdq_sb", 2, [128, 8192], BF16)
    attD = sbnd("attD", 2, [128, 128])

    def p6b():
        for th in range(2):
            for Ic in range(64):
                dma("sync", qiT[Ic], (qint_s, qint_s.ap[Ic * 128:(Ic + 1) * 128,
                                                        th * 1024:(th + 1) * 1024]))
            for oc in range(16):
                wq_ = wdq_sb[oc % 2]
                dma("sync", wq_, (wdq_s, wdq_s.ap[oc * 128:(oc + 1) * 128, :]))
                pb0 = psD[(oc % 2) * 2]
                pb1 = psD[(oc % 2) * 2 + 1]
                for h5 in range(2):
                    bank = pb0 if h5 == 0 else pb1
                    for Ic in range(64):
                        t_mm(bank,
                             (wq_, wq_.ap[:, Ic * 128:(Ic + 1) * 128]),
                             (qiT[Ic], qiT[Ic].ap[:, h5 * 512:(h5 + 1) * 512]),
                             start=(Ic == 0), stop=(Ic == 63))
                stg_ = stg[oc % 2]
                for h5 in range(2):
                    s_act(stg_, pb0 if h5 == 0 else pb1, AF.Copy,
                          scale=(gbcol, gbcol.ap[:, oc:oc + 1]))
                    for t4 in range(4):
                        pt = (psD[4 + (t4 % 2)], psD[4 + (t4 % 2)].ap[:, 0:128])
                        t_tr(pt, (stg_, stg_.ap[:, t4 * 128:(t4 + 1) * 128]))
                        stg2 = attD[t4 % 2]
                        v_copy(stg2, pt)
                        ic = th * 8 + h5 * 4 + t4
                        dma("gpsimd", (pdn_s, pdn_s.ap[ic * 128:(ic + 1) * 128,
                                                       oc * 128:(oc + 1) * 128]), stg2)

    def p7():
        for ic in range(16):
            xt = xst[ic % 2]
            dma("sync", xt, (x1_s, x1_s.ap[ic * 128:(ic + 1) * 128, :]))
            for pp in range(2):
                pcs = (tq[0], tq[0].ap[:, 0:1024])
                dma("sync", pcs, (pdn_s, pdn_s.ap[ic * 128:(ic + 1) * 128,
                                                  pp * 1024:(pp + 1) * 1024]))
                yt = (tq[1], tq[1].ap[:, 0:1024])
                v_tt(yt, (xt, xt.ap[:, pp * 1024:(pp + 1) * 1024]), pcs, OP.add)
                dma("gpsimd", (y_h, y_h.ap[ic * 128:(ic + 1) * 128,
                                           pp * 1024:(pp + 1) * 1024]), yt, is_out=True)

    go(p6b)
    go(p7)
    es_d.close()

    pb.build(blk, sems, dsems)
    es_g.close()
    ni = sum(len(v) for v in pb.items.values())
    return nc, taps, ni


_CACHE = {}


def _host_inputs(L0_attn_x, L0_attn_Wq, L0_attn_Wk, L0_attn_Wv, L0_attn_Wf,
                 L0_attn_Wo, L0_attn_norm_w, L0_ffn_w_gate, L0_ffn_w_up,
                 L0_ffn_w_down, L0_c, L0_wmod):
    f32 = np.float32

    def stripes(W):
        return np.ascontiguousarray(
            np.asarray(W, f32).reshape(16, 128, 16, 128).transpose(2, 1, 0, 3)
            .reshape(2048, 2048))

    shared = {
        "wq": stripes(L0_attn_Wq),
        "wk": stripes(L0_attn_Wk),
        "wv": stripes(L0_attn_Wv),
        "wo": stripes(L0_attn_Wo),
        "wf": np.ascontiguousarray(
            np.asarray(L0_attn_Wf, f32).reshape(16, 128, 16).transpose(1, 0, 2)
            .reshape(128, 256)),
        "wmod": np.ascontiguousarray(np.asarray(L0_wmod, f32)),
        "wg": np.ascontiguousarray(np.asarray(L0_ffn_w_gate, f32)),
        "wu": np.ascontiguousarray(np.asarray(L0_ffn_w_up, f32)),
        "wd": np.ascontiguousarray(np.asarray(L0_ffn_w_down, f32)),
        "nwcol": np.ascontiguousarray(np.asarray(L0_attn_norm_w, f32).reshape(16, 128).T),
        "eye": np.eye(128, dtype=f32),
        "eyeb": np.eye(128, dtype=f32).astype(ml_dtypes.bfloat16),
        "maskinf": np.where(np.arange(128)[:, None] <= np.arange(128)[None, :],
                            0.0, -1e30).astype(f32),
        "ones128": np.ones((128, 128), f32),
        "eye16": np.eye(16, dtype=f32),
        "onescol": np.ones((128, 1), f32),
    }
    xs = np.asarray(L0_attn_x, f32)
    cs = np.asarray(L0_c, f32)
    in_maps = []
    for c in range(N_CORES):
        m = dict(shared)
        m["x"] = np.ascontiguousarray(xs[c])
        m["ccol"] = np.ascontiguousarray(cs[c].reshape(16, 128).T)
        in_maps.append(m)
    return in_maps


def kernel(**inputs):
    in_maps = _host_inputs(**inputs)
    key = ("prog", STOP_AFTER, tuple(DEBUG_TAPS))
    if key not in _CACHE:
        _CACHE[key] = build_program()
    nc, taps, _ni = _CACHE[key]
    res = run_bass_kernel_spmd(nc, in_maps, core_ids=list(range(N_CORES)))
    out = np.stack([np.asarray(r["y"]) for r in res.results], axis=0)
    if DEBUG_TAPS:
        kernel.last_taps = [{k: np.asarray(r["tap_" + k]) for k in DEBUG_TAPS}
                            for r in res.results]
    return out.astype(np.float32)
